# version 18
# baseline (speedup 1.0000x reference)
import contextlib
import numpy as np
import ml_dtypes
import concourse.bass as bass
import concourse.mybir as mybir
from concourse.bass_utils import run_bass_kernel_spmd

F32 = mybir.dt.float32
BF16 = mybir.dt.bfloat16
AF = mybir.ActivationFunctionType
ALU = mybir.AluOpType
AX = mybir.AxisListType

D = 1024
S = 2048
H = 16
HD = 64
NT = 16
NCH = 8
FF = 4096
NEXP = 10
NQ = 4
NE = 8

f32 = np.float32


class Ctr:

    def __init__(self, handle):
        self.h = handle
        self.n = 0

    def inc(self, inst, d=1):
        self.n += d
        inst.then_inc(self.h, d)
        return inst

    def wait(self, eng, val=None):
        eng.wait_ge(self.h, self.n if val is None else val)


def build_program(dbg=frozenset(), phases=("attn", "ffn")):
    nc = bass.Bass("TRN2", target_bir_lowering=False)
    t = lambda name, shape, kind: nc.dram_tensor(name, shape, F32, kind=kind).ap()
    x_h = t("x", [S, D], "ExternalInput")
    wq_h = t("wq", [D, D], "ExternalInput")
    wk_h = t("wk", [D, D], "ExternalInput")
    wv_h = t("wv", [D, D], "ExternalInput")
    wo_h = t("wo", [D, D], "ExternalInput")
    bq_h = t("bq", [1, D], "ExternalInput")
    bk_h = t("bk", [1, D], "ExternalInput")
    bv_h = t("bv", [1, D], "ExternalInput")
    bo_h = t("bo", [1, D], "ExternalInput")
    wfct_h = t("wfct", [NEXP, D, FF], "ExternalInput")
    wprojt_h = t("wprojt", [NEXP, FF, D], "ExternalInput")
    ct_h = t("ct", [D, NE], "ExternalInput")
    rb_h = t("rb", [128, NE], "ExternalInput")
    cosw_h = t("cosw", [128, NT * HD], "ExternalInput")
    ssinw_h = t("ssinw", [128, NT * HD], "ExternalInput")
    ident_h = t("ident", [128, 128], "ExternalInput")
    trimask_h = t("trimask", [128, 128], "ExternalInput")
    y_h = t("y", [S, D], "ExternalOutput")
    dbg_h = {}
    for name, shape in [
        ("h1f", [128, NCH * S]), ("qf", [128, NCH * S]), ("kf", [128, NCH * S]),
        ("vt", [128, NT * (D + H)]), ("ctx", [128, NCH * S]),
        ("h2f", [128, NCH * S]), ("raw", [128, NT * NE]), ("comb", [128, NT * NE]),
        ("stat", [128, 64]), ("x2t", [128, NT * D]),
    ]:
        if name in dbg:
            dbg_h[name] = t("dbg_" + name, shape, "ExternalOutput")
    dbg_src = {}

    es = contextlib.ExitStack()
    with es:
        SB = lambda name, shape, dt=F32: es.enter_context(nc.sbuf_tensor(name, shape, dt))
        sem = lambda name: Ctr(es.enter_context(nc.semaphore(name)))

        ident_b = SB("ident_b", [128, 128], BF16)
        trimask_b = SB("trimask_b", [128, 128], BF16)
        ones_b = SB("ones_b", [128, 128], BF16)
        negf_b = SB("negf_b", [128, 384], BF16)
        bq_b = SB("bq_b", [1, D], BF16)
        bk_b = SB("bk_b", [1, D], BF16)
        bv_b = SB("bv_b", [1, D], BF16)
        bo_b = SB("bo_b", [1, D], BF16)
        rb_s = SB("rb_s", [128, NE], F32)
        ctf_s = SB("ctf_s", [128, NCH * NE], F32)
        chi_s = SB("chi_s", [128, NCH * NE], BF16)
        clo_s = SB("clo_s", [128, NCH * NE], BF16)
        stat_s = SB("stat_s", [128, 64], F32)
        csc_s = SB("csc_s", [128, 16], F32)
        m1_s = SB("m1_s", [128, NT * NE], F32)
        m2_s = SB("m2_s", [128, NT * NE], F32)
        r01_s = SB("r01_s", [128, 2 * NT], F32)
        rs01_s = SB("rs01_s", [128, 2 * NT], F32)
        w0_s = SB("w0_s", [128, NT], F32)
        w1_s = SB("w1_s", [128, NT], F32)
        comb_s = SB("comb_s", [128, NT * NE], F32)
        rawsb_s = SB("rawsb_s", [128, NT * NE], F32)
        bala_s = SB("bala_s", [128, NT * NE], F32)
        dbg_src.update(stat=stat_s, raw=rawsb_s, comb=comb_s)

        wtab = sem("wtab")
        wx = [sem("wxa"), sem("wxb")]
        wx2 = sem("wx2")
        wgt = sem("wgt")
        wg = [sem("wga"), sem("wgb")]
        wgo = sem("wgo")
        pe = sem("pe")
        dv = sem("dv")
        ac = sem("ac")
        out = sem("out")

        block = es.enter_context(nc.Block())

        def dump(names):
            if not any(n in dbg_h for n in names):
                return
            @block.gpsimd
            def _(g):
                dv.wait(g)
                ac.wait(g)
                pe.wait(g)
                for n in names:
                    if n in dbg_h and dbg_src.get(n) is not None:
                        out.inc(g.dma_start(out=dbg_h[n][:], in_=dbg_src[n][:]), 16)

        @block.gpsimd
        def _(g):
            wgt.inc(g.memset(ones_b[:], 1.0), 1)
            wgt.inc(g.memset(negf_b[:], -1e9), 1)
            for dst, src in [(ident_b, ident_h), (trimask_b, trimask_h),
                             (bq_b, bq_h), (bk_b, bk_h), (bv_b, bv_h), (bo_b, bo_h)]:
                wgt.inc(g.dma_start(out=dst[:], in_=src[:]), 16)

        @block.sync
        def _(sy):
            wtab.inc(sy.dma_start(out=rb_s[:], in_=rb_h[:]), 16)
            wtab.inc(sy.dma_start(
                out=ctf_s[:].rearrange("p (c e) -> p c e", c=NCH),
                in_=ct_h.rearrange("(c p) e -> p c e", p=128)), 16)

        es_a2 = es.enter_context(contextlib.ExitStack())
        SB2 = lambda name, shape, dt: es_a2.enter_context(nc.sbuf_tensor(name, shape, dt))
        cos_s = SB2("cos_s", [128, NT * HD], F32)
        ssin_s = SB2("ssin_s", [128, NT * HD], F32)
        qf_s = SB2("qf_s", [128, NCH * S], BF16)
        kf_s = SB2("kf_s", [128, NCH * S], BF16)
        vt_s = SB2("vt_s", [128, NT * (D + H)], BF16)
        dbg_src.update(qf=qf_s, kf=kf_s, vt=vt_s)

        @block.sync
        def _(sy):
            wtab.inc(sy.dma_start(out=cos_s[:], in_=cosw_h[:]), 16)
            wtab.inc(sy.dma_start(out=ssin_s[:], in_=ssinw_h[:]), 16)
        wtab_all = wtab.n

        @block.vector
        def _(v):
            v.wait_ge(wtab.h, wtab_all)
            dv.inc(v.tensor_copy(chi_s[:], ctf_s[:]))
            dv.wait(v)
            dv.inc(v.tensor_tensor(clo_s[:], ctf_s[:], chi_s[:], ALU.subtract))
        dv_csplit = dv.n

        es_a3 = es.enter_context(contextlib.ExitStack())
        SB3 = lambda name, shape, dt: es_a3.enter_context(nc.sbuf_tensor(name, shape, dt))
        PS3 = lambda name, shape, dt: es_a3.enter_context(nc.psum_tensor(name, shape, dt))
        h1f_s = SB3("h1f_s", [128, NCH * S], BF16)
        wqkv_s = [SB3(f"wqkv{i}", [128, NCH * D], BF16) for i in range(2)]
        xin_s = [SB3(f"xin{i}", [128, D], F32) for i in range(2)]
        t1_s = SB3("t1_s", [128, D], F32)
        t2_s = SB3("t2_s", [128, D], F32)
        ht_s = [SB3(f"ht{i}", [128, D], BF16) for i in range(2)]
        rt_s = [SB3(f"rt{i}", [128, D], BF16) for i in range(2)]
        psT = [PS3(f"psT{i}", [128, NCH * 128], BF16) for i in range(2)]
        psW = [PS3(f"psW{i}", [128, D], F32) for i in range(2)]
        dbg_src["h1f"] = h1f_s

        @block.gpsimd
        def _(g):
            wg[0].inc(g.dma_start(
                out=wqkv_s[0][:].rearrange("p (c d) -> p c d", c=NCH),
                in_=wq_h.rearrange("(c p) d -> p c d", p=128)), 16)
            wg[1].inc(g.dma_start(
                out=wqkv_s[1][:].rearrange("p (c d) -> p c d", c=NCH),
                in_=wk_h.rearrange("(c p) d -> p c d", p=128)), 16)
        wg_wq = wg[0].n
        wg_wk = wg[1].n

        dv_sumA = [0] * NT
        ac_sqA = [0] * NT
        for ti in range(NT):
            @block.sync
            def _(sy, ti=ti):
                if ti >= 2:
                    dv.wait(sy, dv_sumA[ti - 2])
                    ac.wait(sy, ac_sqA[ti - 2])
                wx[ti % 2].inc(sy.dma_start(out=xin_s[ti % 2][:],
                                            in_=x_h[128 * ti:128 * (ti + 1), :]), 16)
            wx_xin = wx[ti % 2].n

            @block.vector
            def _(v, ti=ti, wx_xin=wx_xin):
                v.wait_ge(wx[ti % 2].h, wx_xin)
                dv.inc(v.reduce_sum(stat_s[:, ti:ti + 1], xin_s[ti % 2][:], axis=AX.X))
                dv_sumA[ti] = dv.n

            @block.scalar
            def _(s, ti=ti, wx_xin=wx_xin):
                s.wait_ge(wx[ti % 2].h, wx_xin)
                ac.inc(s.activation(out=t1_s[:], in_=xin_s[ti % 2][:], func=AF.Square,
                                    accum_out=stat_s[:, 16 + ti:17 + ti]))
                ac_sqA[ti] = ac.n

        def batched_rstd():
            @block.vector
            def _(v):
                dv.wait(v, dv_sumA[NT - 1])
                ac.wait(v, ac_sqA[NT - 1])
                dv.inc(v.tensor_scalar(stat_s[:, 32:48], stat_s[:, 0:16],
                                       float(1.0 / D), None, ALU.mult))
                dv.wait(v)
                dv.inc(v.tensor_tensor(csc_s[:, 0:16], stat_s[:, 32:48],
                                       stat_s[:, 32:48], ALU.mult))
                dv.wait(v)
                dv.inc(v.tensor_scalar(stat_s[:, 48:64], stat_s[:, 16:32],
                                       float(1.0 / D), None, ALU.mult))
                dv.wait(v)
                dv.inc(v.tensor_tensor(stat_s[:, 48:64], stat_s[:, 48:64],
                                       csc_s[:, 0:16], ALU.subtract))
                dv.wait(v)

            @block.scalar
            def _(s):
                dv.wait(s)
                ac.inc(s.activation(out=stat_s[:, 48:64], in_=stat_s[:, 48:64],
                                    func=AF.Sqrt, bias=f32(1e-5)))

            @block.vector
            def _(v):
                ac.wait(v)
                dv.inc(v.reciprocal(stat_s[:, 48:64], stat_s[:, 48:64]))
                dv.wait(v)

        batched_rstd()
        dv_rstd1 = dv.n

        dv_h1 = [0] * NT
        pe_T = [0] * NT
        ac_h1f = [0] * NT

        def emit_h1T(j):
            @block.tensor
            def _(tt):
                dv.wait(tt, dv_h1[j])
                if j >= 2:
                    ac.wait(tt, ac_h1f[j - 2])
                last = None
                for c in range(NCH):
                    last = tt.transpose(psT[j % 2][:, 128 * c:128 * (c + 1)],
                                        ht_s[j % 2][:, 128 * c:128 * (c + 1)], ident_b[:])
                pe.inc(last)
                pe_T[j] = pe.n

            @block.scalar
            def _(s):
                pe.wait(s, pe_T[j])
                o = h1f_s[:].rearrange("p (c n) -> p c n", c=NCH)[:, :, 128 * j:128 * (j + 1)]
                ac.inc(s.copy(o, psT[j % 2][:].rearrange("p (c n) -> p c n", c=NCH)))
                ac_h1f[j] = ac.n

        for ti in range(NT):
            @block.sync
            def _(sy, ti=ti):
                if ti >= 2:
                    dv.wait(sy, dv_h1[ti - 2])
                else:
                    dv.wait(sy, dv_sumA[NT - 2 + ti])
                    ac.wait(sy, ac_sqA[NT - 2 + ti])
                wx[ti % 2].inc(sy.dma_start(out=xin_s[ti % 2][:],
                                            in_=x_h[128 * ti:128 * (ti + 1), :]), 16)
            wx_xin = wx[ti % 2].n

            @block.vector
            def _(v, ti=ti, wx_xin=wx_xin):
                v.wait_ge(wx[ti % 2].h, wx_xin)
                if ti == 0:
                    dv.wait(v, dv_rstd1)
                elif ti >= 2:
                    pe.wait(v, pe_T[ti - 2])
                dv.inc(v.tensor_scalar(ht_s[ti % 2][:], xin_s[ti % 2][:],
                                       stat_s[:, 32 + ti:33 + ti],
                                       stat_s[:, 48 + ti:49 + ti],
                                       ALU.subtract, ALU.mult))
                dv_h1[ti] = dv.n

            if ti >= 1:
                emit_h1T(ti - 1)
            if ti == NT - 1:
                emit_h1T(NT - 1)

        def cosb(ti, lo, hi):
            return cos_s[:, HD * ti + lo:HD * ti + hi].unsqueeze(1) \
                .broadcast_to([128, H, hi - lo])

        def ssinb(ti, lo, hi):
            return ssin_s[:, HD * ti + lo:HD * ti + hi].unsqueeze(1) \
                .broadcast_to([128, H, hi - lo])

        pe_qkv = [[0] * NT for _ in range(3)]
        dv_rope = [[0] * NT for _ in range(3)]
        ac_rf = [[0] * NT for _ in range(3)]
        for wi in range(3):
            bias = [bq_b, bk_b, bv_b][wi]
            wbuf = wqkv_s[wi % 2]
            if wi == 2:
                @block.gpsimd
                def _(g):
                    pe.wait(g, pe_qkv[0][NT - 1])
                    wg[0].inc(g.dma_start(
                        out=wqkv_s[0][:].rearrange("p (c d) -> p c d", c=NCH),
                        in_=wv_h.rearrange("(c p) d -> p c d", p=128)), 16)
                wready = wg[0].n
                wsem = wg[0]
            else:
                wready = wg_wq if wi == 0 else wg_wk
                wsem = wg[wi]

            pe_rT = [[0] * NT for _ in range(2)]

            def emit_ropeT(j, wi):
                @block.tensor
                def _(tt):
                    dv.wait(tt, dv_rope[wi][j])
                    if j >= 2:
                        ac.wait(tt, ac_rf[wi][j - 2])
                    elif wi == 1:
                        ac.wait(tt, ac_rf[0][NT - 2 + j])
                    last = None
                    for c in range(NCH):
                        last = tt.transpose(psT[j % 2][:, 128 * c:128 * (c + 1)],
                                            rt_s[j % 2][:, 128 * c:128 * (c + 1)],
                                            ident_b[:])
                    pe.inc(last)
                    pe_rT[wi][j] = pe.n

                @block.scalar
                def _(s):
                    pe.wait(s, pe_rT[wi][j])
                    dstf = qf_s if wi == 0 else kf_s
                    o = dstf[:].rearrange("p (c n) -> p c n", c=NCH)[
                        :, :, 128 * j:128 * (j + 1)]
                    ac.inc(s.copy(o, psT[j % 2][:].rearrange("p (c n) -> p c n", c=NCH)))
                    ac_rf[wi][j] = ac.n

            for ti in range(NT):
                @block.tensor
                def _(tt, ti=ti, wi=wi, wbuf=wbuf, wready=wready, wsem=wsem, bias=bias):
                    if ti == 0:
                        wsem.wait(tt, wready)
                        wgt.wait(tt)
                        ac.wait(tt, ac_h1f[NT - 1])
                    if ti >= 2:
                        dv.wait(tt, dv_rope[wi][ti - 2])
                    elif wi > 0:
                        dv.wait(tt, dv_rope[wi - 1][NT - 2 + ti])
                    ps = psW[ti % 2]
                    h1v = h1f_s[:].rearrange("p (c n) -> p c n", c=NCH)
                    for c in range(NCH):
                        lhsT = h1v[:, c, 128 * ti:128 * (ti + 1)]
                        tt.matmul(ps[:, 0:512], lhsT, wbuf[:, 1024 * c:1024 * c + 512],
                                  start=(c == 0), stop=False)
                        tt.matmul(ps[:, 512:1024], lhsT,
                                  wbuf[:, 1024 * c + 512:1024 * (c + 1)],
                                  start=(c == 0), stop=False)
                    tt.matmul(ps[:, 0:512], ones_b[0:1, :], bias[:, 0:512],
                              start=False, stop=True)
                    pe.inc(tt.matmul(ps[:, 512:1024], ones_b[0:1, :], bias[:, 512:1024],
                                     start=False, stop=True))
                    pe_qkv[wi][ti] = pe.n

                if wi < 2:
                    @block.vector
                    def _(v, ti=ti, wi=wi):
                        pe.wait(v, pe_qkv[wi][ti])
                        if ti == 0 and wi == 0:
                            v.wait_ge(wtab.h, wtab_all)
                        if ti >= 2:
                            pe.wait(v, pe_rT[wi][ti - 2])
                        elif wi == 1:
                            pe.wait(v, pe_rT[0][NT - 2 + ti])
                        ps3 = psW[ti % 2][:].rearrange("p (h j) -> p h j", h=H)
                        dv.inc(v.tensor_tensor(
                            t1_s[:].rearrange("p (h j) -> p h j", h=H),
                            ps3, cosb(ti, 0, HD), ALU.mult))
                        t23 = t2_s[:].rearrange("p (h j) -> p h j", h=H)
                        dv.inc(v.tensor_tensor(t23[:, :, 0:32], ps3[:, :, 32:64],
                                               ssinb(ti, 0, 32), ALU.mult))
                        dv.inc(v.tensor_tensor(t23[:, :, 32:64], ps3[:, :, 0:32],
                                               ssinb(ti, 32, 64), ALU.mult))
                        dv.wait(v)
                        dv.inc(v.tensor_tensor(rt_s[ti % 2][:], t1_s[:], t2_s[:], ALU.add))
                        dv_rope[wi][ti] = dv.n

                    if ti >= 1:
                        emit_ropeT(ti - 1, wi)
                    if ti == NT - 1:
                        emit_ropeT(NT - 1, wi)
                else:
                    @block.vector
                    def _(v, ti=ti):
                        pe.wait(v, pe_qkv[2][ti])
                        vtile = vt_s[:, (D + H) * ti:(D + H) * (ti + 1)] \
                            .rearrange("p (h j) -> p h j", j=HD + 1)
                        dv.inc(v.tensor_copy(
                            vtile[:, :, 0:HD],
                            psW[ti % 2][:].rearrange("p (h j) -> p h j", j=HD)))
                        dv.inc(v.memset(vtile[:, :, HD:HD + 1], 1.0))
                        dv_rope[2][ti] = dv.n

        dump(["qf", "kf", "vt"])
        if "qkvstop" in phases:
            @block.sync
            def _(sy):
                sy.wait_ge(out.h, out.n)
            return nc
        es_a3.close()

        es_ctx = es.enter_context(contextlib.ExitStack())
        ctx_s = es_ctx.enter_context(
            nc.sbuf_tensor("ctx_s", [128, NCH * S], BF16, side="right"))
        wo_s = es_ctx.enter_context(
            nc.sbuf_tensor("wo_s", [128, NCH * D], BF16, side="right"))
        dbg_src["ctx"] = ctx_s

        @block.gpsimd
        def _(g):
            pe.wait(g, pe_qkv[2][NT - 1])
            wgo.inc(g.dma_start(
                out=wo_s[:].rearrange("p (c d) -> p c d", c=NCH),
                in_=wo_h.rearrange("(c p) d -> p c d", p=128)), 16)
        wgo_wo = wgo.n

        es_a4 = es.enter_context(contextlib.ExitStack())
        SB4 = lambda name, shape, dt: es_a4.enter_context(nc.sbuf_tensor(name, shape, dt))
        PS4 = lambda name, shape, dt: es_a4.enter_context(nc.psum_tensor(name, shape, dt))
        p_s = SB4("p_s", [128, 2048], BF16)
        ctxsb_s = SB4("ctxsb", [64, 1024], F32)
        nrm_s = SB4("nrm", [64, 1024], BF16)
        zinv_s = SB4("zinv", [128, 1024], BF16)
        psS = PS4("psS", [128, 2048], F32)
        psC = [PS4(f"psC{i}", [128, 1024], F32) for i in range(2)]

        zbs = sem("zbs")
        shs = sem("shs")

        CTX_LAG = 3
        units = [(h, qh) for h in range(H) for qh in range(2)]
        sops = []
        for ui, (h, qh) in enumerate(units):
            kts = list(range(8 * (qh + 1)))
            qbase = 1024 * qh
            last_kt_b = {b: max(kt for kt in kts
                                if max(128 * kt - qbase, 0) < 512 * (b + 1))
                         for b in range(2)}
            for kt in kts:
                rs = max(128 * kt - qbase, 0)
                for b in range(2):
                    if rs >= 512 * (b + 1):
                        continue
                    sops.append(dict(
                        ui=ui, h=h, qh=qh, c=h // 2, r0=(h % 2) * 64,
                        qbase=qbase, kt=kt, b=b, rs=rs,
                        stop=(kt == last_kt_b[b]),
                        unit_first=(kt == 0 and b == 0),
                        unit_last=(kt == kts[-1] and b == 1)))

        NS = len(sops)
        exp_hist = [0] * NS
        ctxmm_hist = [0] * NS
        pe_S = [0] * NS
        pe_ctx = {}
        ac_ctxcp = {}
        dv_nrm = {}
        dv_fin = {}
        n_odd = 0
        last_odd_unit = [None]
        pending_tail = []

        def emit_S(i):
            d = sops[i]

            @block.tensor
            def _(tt):
                if i == 0:
                    ac.wait(tt, ac_rf[1][NT - 1])
                    dv.wait(tt, dv_rope[2][NT - 1])
                if i >= 4:
                    ac.wait(tt, exp_hist[i - 4])
                pso = 512 * (i % 4)
                lhsT = kf_s[d["r0"]:d["r0"] + 64,
                            2048 * d["c"] + 128 * d["kt"]:2048 * d["c"] + 128 * (d["kt"] + 1)]
                lo, hi = max(d["rs"], 512 * d["b"]), 512 * (d["b"] + 1)
                rhs = qf_s[d["r0"]:d["r0"] + 64,
                           2048 * d["c"] + d["qbase"] + lo:2048 * d["c"] + d["qbase"] + hi]
                diag = (128 * d["kt"] >= d["qbase"]) and (512 * d["b"] <= d["rs"])
                pad = lo - 512 * d["b"]
                last = tt.matmul(psS[:, pso + pad:pso + 512], lhsT, rhs, start=True,
                                 stop=not diag, tile_position=(d["r0"], 0))
                if diag:
                    last = tt.matmul(psS[:, pso + pad:pso + pad + 128], ident_b[:],
                                     trimask_b[:], start=False, stop=True)
                if pad:
                    last = tt.matmul(psS[:, pso:pso + pad], ident_b[:],
                                     negf_b[:, 0:pad], start=False, stop=True)
                pe.inc(last)
                pe_S[i] = pe.n

        def emit_exp_pair(i, single=False):
            lo = i if single else i - 1
            off = 512 * (lo % 4)
            width = 512 if single else 1024

            @block.scalar
            def _(s):
                pe.wait(s, pe_S[i])
                if lo >= 4:
                    pe.wait(s, ctxmm_hist[lo - 4])
                ac.inc(s.activation(out=p_s[:, off:off + width],
                                    in_=psS[:, off:off + width],
                                    func=AF.Exp, scale=f32(0.125)))
                for j in range(lo, i + 1):
                    exp_hist[j] = ac.n

        def emit_ctx(i):
            d = sops[i]

            @block.tensor
            def _(tt):
                ac.wait(tt, exp_hist[i])
                if d["unit_first"] and d["ui"] >= 1:
                    u0 = units[d["ui"] - 1]
                    dv.wait(tt, dv_fin[u0])
                vtile = vt_s[:, (D + H) * d["kt"] + 65 * d["h"]:
                             (D + H) * d["kt"] + 65 * (d["h"] + 1)]
                last = tt.matmul(psC[d["ui"] % 2][0:65, 512 * d["b"]:512 * (d["b"] + 1)], vtile,
                                 p_s[:, 512 * (i % 4):512 * (i % 4) + 512],
                                 start=(d["kt"] == 0), stop=d["stop"])
                pe.inc(last)
                ctxmm_hist[i] = pe.n
                if d["unit_last"]:
                    pe_ctx[(d["h"], d["qh"])] = pe.n
            if d["unit_last"]:
                emit_unit_tail(d["ui"], d["h"], d["qh"])

        def emit_unit_tail(ui, h, qh):
            nonlocal n_odd
            c, r0 = h // 2, (h % 2) * 64
            qbase = 1024 * qh
            pc = psC[ui % 2]

            @block.vector
            def _(v):
                pe.wait(v, pe_ctx[(h, qh)])
                if ui >= 1:
                    v.wait_ge(zbs.h, ui)
                with nc.allow_low_precision(reason="1/Z in bf16"):
                    dv.inc(v.reciprocal(zinv_s[64:65, :], pc[64:65, :]))
                dv.inc(v.tensor_copy(ctxsb_s[:], pc[0:64, :]))
                dv.wait(v)
            dv_recip = dv.n

            def tail_zb(dv_recip=dv_recip, pc=pc):
                @block.tensor
                def _(tt):
                    dv.wait(tt, dv_recip)
                    last = None
                    for b in range(2):
                        sl = slice(512 * b, 512 * (b + 1))
                        last = tt.matmul(pc[0:64, sl], ones_b[64:65, 0:64],
                                         zinv_s[64:65, sl],
                                         start=True, stop=True, tile_position=(64, 0))
                    zbs.inc(last)
            pending_tail.append(tail_zb)

            if r0 == 64:
                n_odd += 1
            my_odd = n_odd

            @block.vector
            def _(v):
                v.wait_ge(zbs.h, ui + 1)
                if r0 == 64 and my_odd >= 2:
                    v.wait_ge(shs.h, my_odd - 1)
                dst = (ctx_s[0:64, 2048 * c + qbase:2048 * c + qbase + 1024]
                       if r0 == 0 else nrm_s[:])
                dv.inc(v.tensor_tensor(dst, ctxsb_s[:], pc[0:64, :], ALU.mult))
                dv_nrm[(h, qh)] = dv.n
                dv_fin[(h, qh)] = dv.n

            if r0 == 64:
                def tail_shift(pc=pc):
                    @block.tensor
                    def _(tt):
                        dv.wait(tt, dv_nrm[(h, qh)])
                        last = None
                        for b in range(2):
                            sl = slice(512 * b, 512 * (b + 1))
                            last = tt.matmul(pc[64:128, sl], ident_b[0:64, 0:64],
                                             nrm_s[:, sl], start=True, stop=True,
                                             tile_position=(0, 64))
                        shs.inc(last)
                pending_tail.append(tail_shift)

                @block.vector
                def _(v, my_odd=my_odd):
                    v.wait_ge(shs.h, my_odd)
                    dv.inc(v.tensor_copy(
                        ctx_s[64:128, 2048 * c + qbase:2048 * c + qbase + 1024],
                        pc[64:128, :]))
                    dv_fin[(h, qh)] = dv.n

        for i in range(NS):
            emit_S(i)
            if pending_tail:
                for fn in pending_tail:
                    fn()
                pending_tail = []
            if i % 2 == 1:
                emit_exp_pair(i)
            elif i == NS - 1:
                emit_exp_pair(i, single=True)
            if i - CTX_LAG >= 0:
                emit_ctx(i - CTX_LAG)
        for j in range(NS - CTX_LAG, NS):
            emit_ctx(j)
        for fn in pending_tail:
            fn()
        pending_tail = []
        es_a4.close()


        dump(["ctx"])

        es_a2.close()
        esf1 = es.enter_context(contextlib.ExitStack())
        x2_s = esf1.enter_context(nc.sbuf_tensor("x2_s", [128, NT * D], F32))
        dbg_src["x2t"] = x2_s
        es_a5 = es.enter_context(contextlib.ExitStack())
        SB5 = lambda name, shape, dt: es_a5.enter_context(nc.sbuf_tensor(name, shape, dt))
        PS5 = lambda name, shape, dt: es_a5.enter_context(nc.psum_tensor(name, shape, dt))
        xin2_s = [SB5(f"xin2{i}", [128, D], F32) for i in range(2)]
        psO = [PS5(f"psO{i}", [128, D], F32) for i in range(2)]
        dv_o = [0] * NT
        ac_sq2 = [0] * NT
        for ti in range(NT):
            @block.sync
            def _(sy, ti=ti):
                if ti >= 2:
                    dv.wait(sy, dv_o[ti - 2])
                    ac.wait(sy, ac_sq2[ti - 2])
                else:
                    dv.wait(sy, dv_fin[units[-1]])
                wx[ti % 2].inc(sy.dma_start(out=xin2_s[ti % 2][:],
                                            in_=x_h[128 * ti:128 * (ti + 1), :]), 16)
            wx_x2 = wx[ti % 2].n

            @block.tensor
            def _(tt, ti=ti):
                if ti == 0:
                    wgo.wait(tt, wgo_wo)
                    for u in units:
                        dv.wait(tt, dv_fin[u])
                if ti >= 2:
                    dv.wait(tt, dv_o[ti - 2])
                ps = psO[ti % 2]
                ctxv = ctx_s[:].rearrange("p (c n) -> p c n", c=NCH)
                for c in range(NCH):
                    lhsT = ctxv[:, c, 128 * ti:128 * (ti + 1)]
                    tt.matmul(ps[:, 0:512], lhsT, wo_s[:, 1024 * c:1024 * c + 512],
                              start=(c == 0), stop=False)
                    tt.matmul(ps[:, 512:1024], lhsT,
                              wo_s[:, 1024 * c + 512:1024 * (c + 1)],
                              start=(c == 0), stop=False)
                tt.matmul(ps[:, 0:512], ones_b[0:1, :], bo_b[:, 0:512],
                          start=False, stop=True)
                pe.inc(tt.matmul(ps[:, 512:1024], ones_b[0:1, :], bo_b[:, 512:1024],
                                 start=False, stop=True))
            pe_o = pe.n

            @block.vector
            def _(v, ti=ti, wx_x2=wx_x2, pe_o=pe_o):
                pe.wait(v, pe_o)
                v.wait_ge(wx[ti % 2].h, wx_x2)
                xv = x2_s[:, D * ti:D * (ti + 1)]
                dv.inc(v.tensor_tensor(xv, psO[ti % 2][:], xin2_s[ti % 2][:], ALU.add))
                dv_o[ti] = dv.n
                dv.inc(v.reduce_sum(stat_s[:, ti:ti + 1], xv, axis=AX.X))

            @block.scalar
            def _(s, ti=ti):
                dv.wait(s, dv_o[ti])
                ac.inc(s.activation(out=xin2_s[ti % 2][:],
                                    in_=x2_s[:, D * ti:D * (ti + 1)],
                                    func=AF.Square,
                                    accum_out=stat_s[:, 16 + ti:17 + ti]))
                ac_sq2[ti] = ac.n
        pe_attn_end = pe.n
        dv_sumA2 = dv.n
        ac_sqA2 = ac.n
        dump(["x2t"])
        es_a5.close()
        es_ctx.close()

        if "ffn" in phases:
            esf = es.enter_context(contextlib.ExitStack())
            SBF = lambda name, shape, dt: esf.enter_context(nc.sbuf_tensor(name, shape, dt))
            h2f_s = SBF("h2f_s", [128, NCH * S], BF16)
            wfc_s = [SBF(f"wfc{i}", [128, NCH * D], BF16) for i in range(2)]
            wpj_s = [SBF(f"wpj{i}", [128, NCH * D], BF16) for i in range(2)]
            y_s = SBF("y_s", [128, NCH * D], BF16)
            h2sc_s = [SBF(f"h2sc{i}", [128, D], F32) for i in range(2)]
            lot_s = [SBF(f"lot{i}", [128, D], BF16) for i in range(2)]
            lof_s = [SBF(f"lof{i}", [128, NCH * 128], BF16) for i in range(2)]
            dbg_src["h2f"] = h2f_s

            es_ln2 = es.enter_context(contextlib.ExitStack())
            PSL = lambda name, shape, dt: es_ln2.enter_context(nc.psum_tensor(name, shape, dt))
            psT2h = [PSL(f"psT2h{i}", [128, NCH * 128], BF16) for i in range(2)]
            psT2l = [PSL(f"psT2l{i}", [128, NCH * 128], BF16) for i in range(2)]
            psR = PSL("psR", [128, NE], F32)


            @block.vector
            def _(v):
                dv.wait(v, dv_sumA2)
                ac.wait(v, ac_sqA2)
                dv.inc(v.tensor_scalar(stat_s[:, 32:48], stat_s[:, 0:16],
                                       float(1.0 / D), None, ALU.mult))
                dv.wait(v)
                dv.inc(v.tensor_tensor(csc_s[:, 0:16], stat_s[:, 32:48],
                                       stat_s[:, 32:48], ALU.mult))
                dv.wait(v)
                dv.inc(v.tensor_scalar(stat_s[:, 48:64], stat_s[:, 16:32],
                                       float(1.0 / D), None, ALU.mult))
                dv.wait(v)
                dv.inc(v.tensor_tensor(stat_s[:, 48:64], stat_s[:, 48:64],
                                       csc_s[:, 0:16], ALU.subtract))
                dv.wait(v)

            @block.scalar
            def _(s):
                dv.wait(s)
                ac.inc(s.activation(out=stat_s[:, 48:64], in_=stat_s[:, 48:64],
                                    func=AF.Sqrt, bias=f32(1e-5)))

            @block.vector
            def _(v):
                ac.wait(v)
                dv.inc(v.reciprocal(stat_s[:, 48:64], stat_s[:, 48:64]))
                dv.wait(v)
            dv_rstd2 = dv.n

            dv_h2f = [0] * NT
            ac_hi = [0] * NT
            pe_hiT = [0] * NT
            ac_h2f = [0] * NT
            dv_lo = [0] * NT
            pe_loT = [0] * NT
            ac_lof = [0] * NT
            dv_rawc = [0] * NT
            for ti in range(NT):
                @block.vector
                def _(v, ti=ti):
                    if ti == 0:
                        dv.wait(v, dv_rstd2)
                    elif ti >= 2:
                        ac.wait(v, ac_hi[ti - 2])
                    xv = x2_s[:, D * ti:D * (ti + 1)]
                    dv.inc(v.tensor_scalar(h2sc_s[ti % 2][:], xv,
                                           stat_s[:, 32 + ti:33 + ti],
                                           stat_s[:, 48 + ti:49 + ti],
                                           ALU.subtract, ALU.mult))
                    dv_h2f[ti] = dv.n
                    dv.inc(v.tensor_tensor(xv, xv, h2sc_s[ti % 2][:], ALU.add))

                @block.scalar
                def _(s, ti=ti):
                    dv.wait(s, dv_h2f[ti])
                    if ti >= 2:
                        pe.wait(s, pe_loT[ti - 2])
                    ac.inc(s.copy(lot_s[ti % 2][:], h2sc_s[ti % 2][:]))
                    ac_hi[ti] = ac.n

                @block.tensor
                def _(tt, ti=ti):
                    ac.wait(tt, ac_hi[ti])
                    if ti >= 2:
                        ac.wait(tt, ac_h2f[ti - 2])
                    last = None
                    for c in range(NCH):
                        last = tt.transpose(psT2h[ti % 2][:, 128 * c:128 * (c + 1)],
                                            lot_s[ti % 2][:, 128 * c:128 * (c + 1)],
                                            ident_b[:])
                    pe.inc(last)
                    pe_hiT[ti] = pe.n

                @block.scalar
                def _(s, ti=ti):
                    pe.wait(s, pe_hiT[ti])
                    o = h2f_s[:].rearrange("p (c n) -> p c n", c=NCH)[
                        :, :, 128 * ti:128 * (ti + 1)]
                    ac.inc(s.copy(o, psT2h[ti % 2][:].rearrange("p (c n) -> p c n", c=NCH)))
                    ac_h2f[ti] = ac.n

                @block.vector
                def _(v, ti=ti):
                    pe.wait(v, pe_hiT[ti])
                    dv.inc(v.tensor_tensor(lot_s[ti % 2][:], h2sc_s[ti % 2][:],
                                           lot_s[ti % 2][:], ALU.subtract))
                    dv_lo[ti] = dv.n

                @block.tensor
                def _(tt, ti=ti):
                    dv.wait(tt, dv_lo[ti])
                    if ti >= 2:
                        ac.wait(tt, ac_lof[ti - 2])
                    last = None
                    for c in range(NCH):
                        last = tt.transpose(psT2l[ti % 2][:, 128 * c:128 * (c + 1)],
                                            lot_s[ti % 2][:, 128 * c:128 * (c + 1)],
                                            ident_b[:])
                    pe.inc(last)
                    pe_loT[ti] = pe.n

                @block.scalar
                def _(s, ti=ti):
                    pe.wait(s, pe_loT[ti])
                    ac.inc(s.copy(lof_s[ti % 2][:].rearrange("p (c n) -> p c n", c=NCH),
                                  psT2l[ti % 2][:].rearrange("p (c n) -> p c n", c=NCH)))
                    ac_lof[ti] = ac.n

                @block.tensor
                def _(tt, ti=ti):
                    ac.wait(tt, ac_lof[ti])
                    if ti == 0:
                        dv.wait(tt, dv_csplit)
                    else:
                        dv.wait(tt, dv_rawc[ti - 1])
                    h2v = h2f_s[:].rearrange("p (c n) -> p c n", c=NCH)
                    lov = lof_s[ti % 2][:].rearrange("p (c n) -> p c n", c=NCH)
                    chiv = chi_s[:].rearrange("p (c e) -> p c e", c=NCH)
                    clov = clo_s[:].rearrange("p (c e) -> p c e", c=NCH)
                    last = None
                    for c in range(NCH):
                        hi_t = h2v[:, c, 128 * ti:128 * (ti + 1)]
                        tt.matmul(psR[:], hi_t, chiv[:, c], start=(c == 0), stop=False)
                        tt.matmul(psR[:], hi_t, clov[:, c], start=False, stop=False)
                        last = tt.matmul(psR[:], lov[:, c, :], chiv[:, c], start=False,
                                         stop=(c == NCH - 1))
                    pe.inc(last)
                pe_raw = pe.n

                @block.vector
                def _(v, ti=ti, pe_raw=pe_raw):
                    pe.wait(v, pe_raw)
                    dv.inc(v.tensor_copy(rawsb_s[:, NE * ti:NE * (ti + 1)], psR[:]))
                    dv_rawc[ti] = dv.n

            E3 = lambda t: t[:].rearrange("p (t e) -> p t e", e=NE)

            @block.vector
            def _(v):
                dv.wait(v, dv_rawc[NT - 1])
                rbb = rb_s[:].unsqueeze(1).broadcast_to([128, NT, NE])
                dv.inc(v.tensor_tensor(E3(bala_s), E3(rawsb_s), rbb, ALU.add))
                dv.wait(v)
                dv.inc(v.reduce_max(stat_s[:, 0:16], E3(bala_s), axis=AX.X))
                dv.wait(v)
                mx1 = stat_s[:, 0:16].unsqueeze(2).broadcast_to([128, NT, NE])
                dv.inc(v.tensor_tensor(E3(m1_s), E3(bala_s), mx1, ALU.is_equal))
                dv.wait(v)
                dv.inc(v.scalar_tensor_tensor(bala_s[:], m1_s[:], -1e30, bala_s[:],
                                              ALU.mult, ALU.add))
                dv.wait(v)
                dv.inc(v.reduce_max(stat_s[:, 16:32], E3(bala_s), axis=AX.X))
                dv.wait(v)
                mx2 = stat_s[:, 16:32].unsqueeze(2).broadcast_to([128, NT, NE])
                dv.inc(v.tensor_tensor(E3(m2_s), E3(bala_s), mx2, ALU.is_equal))
                dv.wait(v)
                r2 = r01_s[:].rearrange("p (t two) -> p t two", two=2)
                dv.inc(v.tensor_tensor(bala_s[:], rawsb_s[:], m1_s[:], ALU.mult))
                dv.wait(v)
                dv.inc(v.reduce_sum(r2[:, :, 0:1], E3(bala_s), axis=AX.X))
                dv.wait(v)
                dv.inc(v.tensor_tensor(bala_s[:], rawsb_s[:], m2_s[:], ALU.mult))
                dv.wait(v)
                dv.inc(v.reduce_sum(r2[:, :, 1:2], E3(bala_s), axis=AX.X))
                dv.wait(v)

            @block.scalar
            def _(s):
                dv.wait(s)
                ac.inc(s.activation(out=rs01_s[:], in_=r01_s[:], func=AF.Sigmoid))

            @block.vector
            def _(v):
                ac.wait(v)
                r3 = rs01_s[:].rearrange("p (t two) -> p t two", two=2)
                dv.inc(v.tensor_tensor(w1_s[:].unsqueeze(2), r3[:, :, 0:1],
                                       r3[:, :, 1:2], ALU.subtract))
                dv.wait(v)

            @block.scalar
            def _(s):
                dv.wait(s)
                ac.inc(s.activation(out=w0_s[:], in_=w1_s[:], func=AF.Sigmoid))

            @block.vector
            def _(v):
                ac.wait(v)
                dv.inc(v.tensor_scalar(w1_s[:], w0_s[:], -1.0, 1.0, ALU.mult, ALU.add))
                dv.wait(v)
                w0b = w0_s[:].unsqueeze(2).broadcast_to([128, NT, NE])
                w1b = w1_s[:].unsqueeze(2).broadcast_to([128, NT, NE])
                dv.inc(v.tensor_tensor(E3(comb_s), E3(m1_s), w0b, ALU.mult))
                dv.wait(v)
                dv.inc(v.tensor_tensor(E3(bala_s), E3(m2_s), w1b, ALU.mult))
                dv.wait(v)
                dv.inc(v.tensor_tensor(comb_s[:], comb_s[:], bala_s[:], ALU.add))
                dv.wait(v)
            dv_comb = dv.n
            dump(["h2f", "raw", "comb"])
            es_ln2.close()

            esq = es.enter_context(contextlib.ExitStack())
            PSQ = lambda name, shape, dt: esq.enter_context(nc.psum_tensor(name, shape, dt))
            psF = [PSQ(f"psF{i}", [128, D], F32) for i in range(2)]
            psP = [PSQ(f"psP{i}", [128, D], F32) for i in range(2)]

            wq_done = {}

            @block.gpsimd
            def _(g):
                pe.wait(g, pe_attn_end)
                wg[0].inc(g.dma_start(
                    out=wfc_s[0][:].rearrange("p (c f) -> p c f", c=NCH),
                    in_=wfct_h[0].rearrange("(c p) f -> p c f", p=128)[:, :, 0:1024]), 16)
                wg[0].inc(g.dma_start(
                    out=wpj_s[0][:].rearrange("p (j o) -> p j o", j=NCH),
                    in_=wprojt_h[0].rearrange("(j p) o -> p j o", p=128)[:, 0:NCH, :]), 16)
                wq_done[(0, 0)] = wg[0].n
                wg[1].inc(g.dma_start(
                    out=wfc_s[1][:].rearrange("p (c f) -> p c f", c=NCH),
                    in_=wfct_h[0].rearrange("(c p) f -> p c f", p=128)[:, :, 1024:2048]), 16)
                wg[1].inc(g.dma_start(
                    out=wpj_s[1][:].rearrange("p (j o) -> p j o", j=NCH),
                    in_=wprojt_h[0].rearrange("(j p) o -> p j o", p=128)[:, NCH:2 * NCH, :]), 16)
                wq_done[(0, 1)] = wg[1].n

            gelu_hist = {}
            evac_hist = {}
            for e in range(NEXP):
                for q in range(NQ):
                    qi = e * NQ + q
                    buf = qi % 2
                    for th in range(2):
                        for jp in range(0, NCH, 2):
                            @block.tensor
                            def _(tt, e=e, q=q, qi=qi, th=th, buf=buf, jp=jp):
                                if jp == 0 and th == 0:
                                    wg[buf].wait(tt, wq_done[(e, q)])
                                    if qi == 0:
                                        ac.wait(tt, ac_lof[NT - 1])
                                        dv.wait(tt, dv_rawc[NT - 1])
                                h2v = h2f_s[:].rearrange("p (c n) -> p c n", c=NCH)
                                for j in (jp, jp + 1):
                                    if j >= 2:
                                        key = (qi, th, j - 2)
                                    elif th == 1:
                                        key = (qi, 0, j + 6)
                                    elif qi > 0:
                                        key = (qi - 1, 1, j + 6)
                                    else:
                                        key = None
                                    if key is not None and key in gelu_hist:
                                        ac.wait(tt, gelu_hist[key])
                                    ps = psF[j % 2]
                                    last = None
                                    for cc in range(NCH):
                                        lhsT = wfc_s[buf][:, 1024 * cc + 128 * j:
                                                          1024 * cc + 128 * (j + 1)]
                                        for tc in range(2):
                                            rhs = h2v[:, cc, 1024 * th + 512 * tc:
                                                      1024 * th + 512 * (tc + 1)]
                                            last = tt.matmul(
                                                ps[:, 512 * tc:512 * (tc + 1)], lhsT, rhs,
                                                start=(cc == 0), stop=(cc == NCH - 1))
                                    pe.inc(last)
                            pe_fc_pair = pe.n

                            @block.scalar
                            def _(s, qi=qi, th=th, jp=jp, pe_fc_pair=pe_fc_pair):
                                for j in (jp, jp + 1):
                                    pe.wait(s, pe_fc_pair - (1 if j == jp else 0))
                                    ac.inc(s.activation(
                                        out=y_s[:, 1024 * j:1024 * (j + 1)],
                                        in_=psF[j % 2][:], func=AF.Gelu))
                                    gelu_hist[(qi, th, j)] = ac.n

                        for tp in range(0, 8, 2):
                            @block.tensor
                            def _(tt, qi=qi, th=th, buf=buf, tp=tp):
                                ac.wait(tt, gelu_hist[(qi, th, NCH - 1)])
                                for tloc in (tp, tp + 1):
                                    if tloc >= 2:
                                        key = (qi, th, tloc - 2)
                                    elif th == 1:
                                        key = (qi, 0, tloc + 6)
                                    elif qi > 0:
                                        key = (qi - 1, 1, tloc + 6)
                                    else:
                                        key = None
                                    if key is not None and key in evac_hist:
                                        dv.wait(tt, evac_hist[key])
                                    pps = psP[tloc % 2]
                                    last = None
                                    for j in range(NCH):
                                        lhsT = y_s[:, 1024 * j + 128 * tloc:
                                                   1024 * j + 128 * (tloc + 1)]
                                        for dh in range(2):
                                            last = tt.matmul(
                                                pps[:, 512 * dh:512 * (dh + 1)], lhsT,
                                                wpj_s[buf][:, 1024 * j + 512 * dh:
                                                           1024 * j + 512 * (dh + 1)],
                                                start=(j == 0), stop=(j == NCH - 1))
                                    pe.inc(last)
                            pe_pj_pair = pe.n

                            @block.vector
                            def _(v, e=e, qi=qi, th=th, tp=tp, pe_pj_pair=pe_pj_pair):
                                for tloc in (tp, tp + 1):
                                    pe.wait(v, pe_pj_pair - (1 if tloc == tp else 0))
                                    ti = 8 * th + tloc
                                    xv = x2_s[:, D * ti:D * (ti + 1)]
                                    pps = psP[tloc % 2]
                                    if e < 2:
                                        dv.inc(v.tensor_tensor(xv, xv, pps[:], ALU.add))
                                    else:
                                        cc = comb_s[:, NE * ti + (e - 2):
                                                    NE * ti + (e - 2) + 1]
                                        dv.inc(v.scalar_tensor_tensor(
                                            xv, pps[:], cc, xv, ALU.mult, ALU.add))
                                    evac_hist[(qi, th, tloc)] = dv.n

                            if e == NEXP - 1 and q == NQ - 1:
                                @block.sync
                                def _(sy, th=th, tp=tp):
                                    for tloc in (tp, tp + 1):
                                        ti = 8 * th + tloc
                                        dv.wait(sy, evac_hist[(NEXP * NQ - 1, th, tloc)])
                                        out.inc(sy.dma_start(
                                            out=y_h[128 * ti:128 * (ti + 1), :],
                                            in_=x2_s[:, D * ti:D * (ti + 1)]), 16)

                    nqi = qi + 2
                    if nqi < NEXP * NQ:
                        ne_, nq_ = divmod(nqi, NQ)

                        @block.gpsimd
                        def _(g, ne_=ne_, nq_=nq_, buf=buf, n=pe.n):
                            pe.wait(g, n)
                            wg[buf].inc(g.dma_start(
                                out=wfc_s[buf][:].rearrange("p (c f) -> p c f", c=NCH),
                                in_=wfct_h[ne_].rearrange("(c p) f -> p c f", p=128)[
                                    :, :, 1024 * nq_:1024 * (nq_ + 1)]), 16)
                            wg[buf].inc(g.dma_start(
                                out=wpj_s[buf][:].rearrange("p (j o) -> p j o", j=NCH),
                                in_=wprojt_h[ne_].rearrange("(j p) o -> p j o", p=128)[
                                    :, NCH * nq_:NCH * (nq_ + 1), :]), 16)
                            wq_done[(ne_, nq_)] = wg[buf].n

            esq.close()
            esf.close()

        @block.sync
        def _(sy):
            dv.wait(sy)
            ac.wait(sy)
            pe.wait(sy)
            sy.wait_ge(out.h, out.n)

    return nc


def make_tables():
    inv_freq = np.exp(np.arange(0, HD, 2, dtype=np.float32)
                      * f32(-np.log(10000.0) / HD)).astype(np.float32)
    pos = np.arange(S, dtype=np.float32)
    ang = (pos[:, None] * inv_freq[None, :]).astype(np.float32)
    sin = np.sin(ang).astype(np.float32)
    cos = np.cos(ang).astype(np.float32)
    cosw = np.zeros((128, NT * HD), np.float32)
    ssinw = np.zeros((128, NT * HD), np.float32)
    for ti in range(NT):
        c = cos[128 * ti:128 * (ti + 1)]
        s_ = sin[128 * ti:128 * (ti + 1)]
        cosw[:, HD * ti:HD * ti + 32] = c
        cosw[:, HD * ti + 32:HD * (ti + 1)] = c
        ssinw[:, HD * ti:HD * ti + 32] = -s_
        ssinw[:, HD * ti + 32:HD * (ti + 1)] = s_
    return cosw, ssinw


def prep_shared(inputs):
    cosw, ssinw = make_tables()
    wfct = np.concatenate([np.asarray(inputs["L0_ffn_Wfc_sh"]),
                           np.asarray(inputs["L0_ffn_Wfc_r"])], 0)
    wfct = np.ascontiguousarray(wfct.transpose(0, 2, 1)).astype(np.float32)
    wprojt = np.concatenate([np.asarray(inputs["L0_ffn_Wproj_sh"]),
                             np.asarray(inputs["L0_ffn_Wproj_r"])], 0)
    wprojt = np.ascontiguousarray(wprojt.transpose(0, 2, 1)).astype(np.float32)
    return dict(
        wq=np.ascontiguousarray(np.asarray(inputs["L0_attn_Wq"], np.float32)),
        wk=np.ascontiguousarray(np.asarray(inputs["L0_attn_Wk"], np.float32)),
        wv=np.ascontiguousarray(np.asarray(inputs["L0_attn_Wv"], np.float32)),
        wo=np.ascontiguousarray(np.asarray(inputs["L0_attn_Wo"], np.float32)),
        bq=np.asarray(inputs["L0_attn_bq"], np.float32).reshape(1, D),
        bk=np.asarray(inputs["L0_attn_bk"], np.float32).reshape(1, D),
        bv=np.asarray(inputs["L0_attn_bv"], np.float32).reshape(1, D),
        bo=np.asarray(inputs["L0_attn_bo"], np.float32).reshape(1, D),
        wfct=wfct, wprojt=wprojt,
        ct=np.ascontiguousarray(np.asarray(inputs["L0_ffn_centroids"], np.float32).T),
        rb=np.ascontiguousarray(np.broadcast_to(
            np.asarray(inputs["L0_ffn_routing_bias"], np.float32)[None, :], (128, NE))),
        cosw=cosw, ssinw=ssinw,
        ident=np.eye(128, dtype=np.float32),
        trimask=np.where(np.triu(np.ones((128, 128), bool)), 0.0, -1e9).astype(np.float32),
    )


def kernel(**inputs):
    nc = build_program()
    shared = prep_shared(inputs)
    xfull = np.asarray(inputs["L0_attn_x"], np.float32)
    in_maps = []
    for b in range(8):
        d = dict(shared)
        d["x"] = np.ascontiguousarray(xfull[b])
        in_maps.append(d)
    res = run_bass_kernel_spmd(nc, in_maps, core_ids=list(range(8)))
    return np.stack([np.asarray(r["y"]) for r in res.results], 0)


# version 19 (final)
# speedup vs baseline: 1.1084x; 1.1084x over previous
import contextlib
import numpy as np
import ml_dtypes
import concourse.bass as bass
import concourse.mybir as mybir
from concourse.bass_utils import run_bass_kernel_spmd

F32 = mybir.dt.float32
BF16 = mybir.dt.bfloat16
AF = mybir.ActivationFunctionType
ALU = mybir.AluOpType
AX = mybir.AxisListType

D = 1024
S = 2048
H = 16
HD = 64
NT = 16
NCH = 8
FF = 4096
NEXP = 10
NQ = 4
NE = 8

f32 = np.float32


class Ctr:

    def __init__(self, handle):
        self.h = handle
        self.n = 0

    def inc(self, inst, d=1):
        self.n += d
        inst.then_inc(self.h, d)
        return inst

    def wait(self, eng, val=None):
        eng.wait_ge(self.h, self.n if val is None else val)


def build_program(dbg=frozenset(), phases=("attn", "ffn")):
    nc = bass.Bass("TRN2", target_bir_lowering=False)
    t = lambda name, shape, kind: nc.dram_tensor(name, shape, F32, kind=kind).ap()
    x_h = t("x", [S, D], "ExternalInput")
    wq_h = t("wq", [D, D], "ExternalInput")
    wk_h = t("wk", [D, D], "ExternalInput")
    wv_h = t("wv", [D, D], "ExternalInput")
    wo_h = t("wo", [D, D], "ExternalInput")
    bq_h = t("bq", [1, D], "ExternalInput")
    bk_h = t("bk", [1, D], "ExternalInput")
    bv_h = t("bv", [1, D], "ExternalInput")
    bo_h = t("bo", [1, D], "ExternalInput")
    wfct_h = t("wfct", [NEXP, D, FF], "ExternalInput")
    wprojt_h = t("wprojt", [NEXP, FF, D], "ExternalInput")
    ct_h = t("ct", [D, NE], "ExternalInput")
    rb_h = t("rb", [128, NE], "ExternalInput")
    cosw_h = t("cosw", [128, NT * HD], "ExternalInput")
    ssinw_h = t("ssinw", [128, NT * HD], "ExternalInput")
    ident_h = t("ident", [128, 128], "ExternalInput")
    trimask_h = t("trimask", [128, 128], "ExternalInput")
    y_h = t("y", [S, D], "ExternalOutput")
    dbg_h = {}
    for name, shape in [
        ("h1f", [128, NCH * S]), ("qf", [128, NCH * S]), ("kf", [128, NCH * S]),
        ("vt", [128, NT * (D + H)]), ("ctx", [128, NCH * S]),
        ("h2f", [128, NCH * S]), ("raw", [128, NT * NE]), ("comb", [128, NT * NE]),
        ("stat", [128, 64]), ("x2t", [128, NT * D]),
    ]:
        if name in dbg:
            dbg_h[name] = t("dbg_" + name, shape, "ExternalOutput")
    dbg_src = {}

    es = contextlib.ExitStack()
    with es:
        SB = lambda name, shape, dt=F32: es.enter_context(nc.sbuf_tensor(name, shape, dt))
        sem = lambda name: Ctr(es.enter_context(nc.semaphore(name)))

        ident_b = SB("ident_b", [128, 128], BF16)
        trimask_b = SB("trimask_b", [128, 128], BF16)
        ones_b = SB("ones_b", [128, 128], BF16)
        negf_b = SB("negf_b", [128, 384], BF16)
        bq_b = SB("bq_b", [1, D], BF16)
        bk_b = SB("bk_b", [1, D], BF16)
        bv_b = SB("bv_b", [1, D], BF16)
        bo_b = SB("bo_b", [1, D], BF16)
        rb_s = SB("rb_s", [128, NE], F32)
        ctf_s = SB("ctf_s", [128, NCH * NE], F32)
        chi_s = SB("chi_s", [128, NCH * NE], BF16)
        clo_s = SB("clo_s", [128, NCH * NE], BF16)
        stat_s = SB("stat_s", [128, 64], F32)
        csc_s = SB("csc_s", [128, 16], F32)
        m1_s = SB("m1_s", [128, NT * NE], F32)
        m2_s = SB("m2_s", [128, NT * NE], F32)
        r01_s = SB("r01_s", [128, 2 * NT], F32)
        rs01_s = SB("rs01_s", [128, 2 * NT], F32)
        w0_s = SB("w0_s", [128, NT], F32)
        w1_s = SB("w1_s", [128, NT], F32)
        comb_s = SB("comb_s", [128, NT * NE], F32)
        rawsb_s = SB("rawsb_s", [128, NT * NE], F32)
        bala_s = SB("bala_s", [128, NT * NE], F32)
        dbg_src.update(stat=stat_s, raw=rawsb_s, comb=comb_s)

        wtab = sem("wtab")
        wx = [sem("wxa"), sem("wxb")]
        wx2 = sem("wx2")
        wgt = sem("wgt")
        wms = sem("wms")
        wg = [sem("wga"), sem("wgb")]
        wgo = sem("wgo")
        pe = sem("pe")
        dv = sem("dv")
        ac = sem("ac")
        out = sem("out")

        block = es.enter_context(nc.Block())

        def dump(names):
            if not any(n in dbg_h for n in names):
                return
            @block.gpsimd
            def _(g):
                dv.wait(g)
                ac.wait(g)
                pe.wait(g)
                for n in names:
                    if n in dbg_h and dbg_src.get(n) is not None:
                        out.inc(g.dma_start(out=dbg_h[n][:], in_=dbg_src[n][:]), 16)

        @block.gpsimd
        def _(g):
            wms.inc(g.memset(ones_b[:], 1.0), 1)
            wms.inc(g.memset(negf_b[:], -1e9), 1)
            for dst, src in [(ident_b, ident_h), (trimask_b, trimask_h),
                             (bq_b, bq_h), (bk_b, bk_h), (bv_b, bv_h), (bo_b, bo_h)]:
                wgt.inc(g.dma_start(out=dst[:], in_=src[:]), 16)

        @block.sync
        def _(sy):
            wtab.inc(sy.dma_start(out=rb_s[:], in_=rb_h[:]), 16)
            wtab.inc(sy.dma_start(
                out=ctf_s[:].rearrange("p (c e) -> p c e", c=NCH),
                in_=ct_h.rearrange("(c p) e -> p c e", p=128)), 16)

        es_a2 = es.enter_context(contextlib.ExitStack())
        SB2 = lambda name, shape, dt: es_a2.enter_context(nc.sbuf_tensor(name, shape, dt))
        cos_s = SB2("cos_s", [128, NT * HD], F32)
        ssin_s = SB2("ssin_s", [128, NT * HD], F32)
        qf_s = SB2("qf_s", [128, NCH * S], BF16)
        kf_s = SB2("kf_s", [128, NCH * S], BF16)
        vt_s = SB2("vt_s", [128, NT * (D + H)], BF16)
        dbg_src.update(qf=qf_s, kf=kf_s, vt=vt_s)

        @block.sync
        def _(sy):
            wtab.inc(sy.dma_start(out=cos_s[:], in_=cosw_h[:]), 16)
            wtab.inc(sy.dma_start(out=ssin_s[:], in_=ssinw_h[:]), 16)
        wtab_all = wtab.n

        @block.vector
        def _(v):
            v.wait_ge(wtab.h, wtab_all)
            dv.inc(v.tensor_copy(chi_s[:], ctf_s[:]))
            dv.wait(v)
            dv.inc(v.tensor_tensor(clo_s[:], ctf_s[:], chi_s[:], ALU.subtract))
        dv_csplit = dv.n

        es_a3 = es.enter_context(contextlib.ExitStack())
        SB3 = lambda name, shape, dt: es_a3.enter_context(nc.sbuf_tensor(name, shape, dt))
        PS3 = lambda name, shape, dt: es_a3.enter_context(nc.psum_tensor(name, shape, dt))
        h1f_s = SB3("h1f_s", [128, NCH * S], BF16)
        wqkv_s = [SB3(f"wqkv{i}", [128, NCH * D], BF16) for i in range(2)]
        xin_s = [SB3(f"xin{i}", [128, D], F32) for i in range(2)]
        t1_s = SB3("t1_s", [128, D], F32)
        t2_s = SB3("t2_s", [128, D], F32)
        ht_s = [SB3(f"ht{i}", [128, D], BF16) for i in range(2)]
        rt_s = [SB3(f"rt{i}", [128, D], BF16) for i in range(2)]
        psT = [PS3(f"psT{i}", [128, NCH * 128], BF16) for i in range(2)]
        psW = [PS3(f"psW{i}", [128, D], F32) for i in range(2)]
        dbg_src["h1f"] = h1f_s

        @block.gpsimd
        def _(g):
            wg[0].inc(g.dma_start(
                out=wqkv_s[0][:].rearrange("p (c d) -> p c d", c=NCH),
                in_=wq_h.rearrange("(c p) d -> p c d", p=128)), 16)
            wg[1].inc(g.dma_start(
                out=wqkv_s[1][:].rearrange("p (c d) -> p c d", c=NCH),
                in_=wk_h.rearrange("(c p) d -> p c d", p=128)), 16)
        wg_wq = wg[0].n
        wg_wk = wg[1].n

        dv_sumA = [0] * NT
        ac_sqA = [0] * NT
        for ti in range(NT):
            @block.sync
            def _(sy, ti=ti):
                if ti >= 2:
                    dv.wait(sy, dv_sumA[ti - 2])
                    ac.wait(sy, ac_sqA[ti - 2])
                wx[ti % 2].inc(sy.dma_start(out=xin_s[ti % 2][:],
                                            in_=x_h[128 * ti:128 * (ti + 1), :]), 16)
            wx_xin = wx[ti % 2].n

            @block.vector
            def _(v, ti=ti, wx_xin=wx_xin):
                v.wait_ge(wx[ti % 2].h, wx_xin)
                dv.inc(v.reduce_sum(stat_s[:, ti:ti + 1], xin_s[ti % 2][:], axis=AX.X))
                dv_sumA[ti] = dv.n

            @block.scalar
            def _(s, ti=ti, wx_xin=wx_xin):
                s.wait_ge(wx[ti % 2].h, wx_xin)
                ac.inc(s.activation(out=t1_s[:], in_=xin_s[ti % 2][:], func=AF.Square,
                                    accum_out=stat_s[:, 16 + ti:17 + ti]))
                ac_sqA[ti] = ac.n

        def batched_rstd():
            @block.vector
            def _(v):
                dv.wait(v, dv_sumA[NT - 1])
                ac.wait(v, ac_sqA[NT - 1])
                dv.inc(v.tensor_scalar(stat_s[:, 32:48], stat_s[:, 0:16],
                                       float(1.0 / D), None, ALU.mult))
                dv.wait(v)
                dv.inc(v.tensor_tensor(csc_s[:, 0:16], stat_s[:, 32:48],
                                       stat_s[:, 32:48], ALU.mult))
                dv.wait(v)
                dv.inc(v.tensor_scalar(stat_s[:, 48:64], stat_s[:, 16:32],
                                       float(1.0 / D), None, ALU.mult))
                dv.wait(v)
                dv.inc(v.tensor_tensor(stat_s[:, 48:64], stat_s[:, 48:64],
                                       csc_s[:, 0:16], ALU.subtract))
                dv.wait(v)

            @block.scalar
            def _(s):
                dv.wait(s)
                ac.inc(s.activation(out=stat_s[:, 48:64], in_=stat_s[:, 48:64],
                                    func=AF.Sqrt, bias=f32(1e-5)))

            @block.vector
            def _(v):
                ac.wait(v)
                dv.inc(v.reciprocal(stat_s[:, 48:64], stat_s[:, 48:64]))
                dv.wait(v)

        batched_rstd()
        dv_rstd1 = dv.n

        dv_h1 = [0] * NT
        pe_T = [0] * NT
        ac_h1f = [0] * NT

        def emit_h1T(j):
            @block.tensor
            def _(tt):
                dv.wait(tt, dv_h1[j])
                if j >= 2:
                    ac.wait(tt, ac_h1f[j - 2])
                last = None
                for c in range(NCH):
                    last = tt.transpose(psT[j % 2][:, 128 * c:128 * (c + 1)],
                                        ht_s[j % 2][:, 128 * c:128 * (c + 1)], ident_b[:])
                pe.inc(last)
                pe_T[j] = pe.n

            @block.scalar
            def _(s):
                pe.wait(s, pe_T[j])
                o = h1f_s[:].rearrange("p (c n) -> p c n", c=NCH)[:, :, 128 * j:128 * (j + 1)]
                ac.inc(s.copy(o, psT[j % 2][:].rearrange("p (c n) -> p c n", c=NCH)))
                ac_h1f[j] = ac.n

        for ti in range(NT):
            @block.sync
            def _(sy, ti=ti):
                if ti >= 2:
                    dv.wait(sy, dv_h1[ti - 2])
                else:
                    dv.wait(sy, dv_sumA[NT - 2 + ti])
                    ac.wait(sy, ac_sqA[NT - 2 + ti])
                wx[ti % 2].inc(sy.dma_start(out=xin_s[ti % 2][:],
                                            in_=x_h[128 * ti:128 * (ti + 1), :]), 16)
            wx_xin = wx[ti % 2].n

            @block.vector
            def _(v, ti=ti, wx_xin=wx_xin):
                v.wait_ge(wx[ti % 2].h, wx_xin)
                if ti == 0:
                    dv.wait(v, dv_rstd1)
                elif ti >= 2:
                    pe.wait(v, pe_T[ti - 2])
                dv.inc(v.tensor_scalar(ht_s[ti % 2][:], xin_s[ti % 2][:],
                                       stat_s[:, 32 + ti:33 + ti],
                                       stat_s[:, 48 + ti:49 + ti],
                                       ALU.subtract, ALU.mult))
                dv_h1[ti] = dv.n

            if ti >= 1:
                emit_h1T(ti - 1)
            if ti == NT - 1:
                emit_h1T(NT - 1)

        def cosb(ti, lo, hi):
            return cos_s[:, HD * ti + lo:HD * ti + hi].unsqueeze(1) \
                .broadcast_to([128, H, hi - lo])

        def ssinb(ti, lo, hi):
            return ssin_s[:, HD * ti + lo:HD * ti + hi].unsqueeze(1) \
                .broadcast_to([128, H, hi - lo])

        pe_qkv = [[0] * NT for _ in range(3)]
        dv_rope = [[0] * NT for _ in range(3)]
        ac_rf = [[0] * NT for _ in range(3)]
        for wi in range(3):
            bias = [bq_b, bk_b, bv_b][wi]
            wbuf = wqkv_s[wi % 2]
            if wi == 2:
                @block.gpsimd
                def _(g):
                    pe.wait(g, pe_qkv[0][NT - 1])
                    wg[0].inc(g.dma_start(
                        out=wqkv_s[0][:].rearrange("p (c d) -> p c d", c=NCH),
                        in_=wv_h.rearrange("(c p) d -> p c d", p=128)), 16)
                wready = wg[0].n
                wsem = wg[0]
            else:
                wready = wg_wq if wi == 0 else wg_wk
                wsem = wg[wi]

            pe_rT = [[0] * NT for _ in range(2)]

            def emit_ropeT(j, wi):
                @block.tensor
                def _(tt):
                    dv.wait(tt, dv_rope[wi][j])
                    if j >= 2:
                        ac.wait(tt, ac_rf[wi][j - 2])
                    elif wi == 1:
                        ac.wait(tt, ac_rf[0][NT - 2 + j])
                    last = None
                    for c in range(NCH):
                        last = tt.transpose(psT[j % 2][:, 128 * c:128 * (c + 1)],
                                            rt_s[j % 2][:, 128 * c:128 * (c + 1)],
                                            ident_b[:])
                    pe.inc(last)
                    pe_rT[wi][j] = pe.n

                @block.scalar
                def _(s):
                    pe.wait(s, pe_rT[wi][j])
                    dstf = qf_s if wi == 0 else kf_s
                    o = dstf[:].rearrange("p (c n) -> p c n", c=NCH)[
                        :, :, 128 * j:128 * (j + 1)]
                    ac.inc(s.copy(o, psT[j % 2][:].rearrange("p (c n) -> p c n", c=NCH)))
                    ac_rf[wi][j] = ac.n

            for ti in range(NT):
                @block.tensor
                def _(tt, ti=ti, wi=wi, wbuf=wbuf, wready=wready, wsem=wsem, bias=bias):
                    if ti == 0:
                        wsem.wait(tt, wready)
                        wgt.wait(tt)
                        wms.wait(tt)
                        ac.wait(tt, ac_h1f[NT - 1])
                    if ti >= 2:
                        dv.wait(tt, dv_rope[wi][ti - 2])
                    elif wi > 0:
                        dv.wait(tt, dv_rope[wi - 1][NT - 2 + ti])
                    ps = psW[ti % 2]
                    h1v = h1f_s[:].rearrange("p (c n) -> p c n", c=NCH)
                    for c in range(NCH):
                        lhsT = h1v[:, c, 128 * ti:128 * (ti + 1)]
                        tt.matmul(ps[:, 0:512], lhsT, wbuf[:, 1024 * c:1024 * c + 512],
                                  start=(c == 0), stop=False)
                        tt.matmul(ps[:, 512:1024], lhsT,
                                  wbuf[:, 1024 * c + 512:1024 * (c + 1)],
                                  start=(c == 0), stop=False)
                    tt.matmul(ps[:, 0:512], ones_b[0:1, :], bias[:, 0:512],
                              start=False, stop=True)
                    pe.inc(tt.matmul(ps[:, 512:1024], ones_b[0:1, :], bias[:, 512:1024],
                                     start=False, stop=True))
                    pe_qkv[wi][ti] = pe.n

                if wi < 2:
                    @block.vector
                    def _(v, ti=ti, wi=wi):
                        pe.wait(v, pe_qkv[wi][ti])
                        if ti == 0 and wi == 0:
                            v.wait_ge(wtab.h, wtab_all)
                        if ti >= 2:
                            pe.wait(v, pe_rT[wi][ti - 2])
                        elif wi == 1:
                            pe.wait(v, pe_rT[0][NT - 2 + ti])
                        ps3 = psW[ti % 2][:].rearrange("p (h j) -> p h j", h=H)
                        dv.inc(v.tensor_tensor(
                            t1_s[:].rearrange("p (h j) -> p h j", h=H),
                            ps3, cosb(ti, 0, HD), ALU.mult))
                        t23 = t2_s[:].rearrange("p (h j) -> p h j", h=H)
                        dv.inc(v.tensor_tensor(t23[:, :, 0:32], ps3[:, :, 32:64],
                                               ssinb(ti, 0, 32), ALU.mult))
                        dv.inc(v.tensor_tensor(t23[:, :, 32:64], ps3[:, :, 0:32],
                                               ssinb(ti, 32, 64), ALU.mult))
                        dv.wait(v)
                        dv.inc(v.tensor_tensor(rt_s[ti % 2][:], t1_s[:], t2_s[:], ALU.add))
                        dv_rope[wi][ti] = dv.n

                    if ti >= 1:
                        emit_ropeT(ti - 1, wi)
                    if ti == NT - 1:
                        emit_ropeT(NT - 1, wi)
                else:
                    @block.vector
                    def _(v, ti=ti):
                        pe.wait(v, pe_qkv[2][ti])
                        vtile = vt_s[:, (D + H) * ti:(D + H) * (ti + 1)] \
                            .rearrange("p (h j) -> p h j", j=HD + 1)
                        dv.inc(v.tensor_copy(
                            vtile[:, :, 0:HD],
                            psW[ti % 2][:].rearrange("p (h j) -> p h j", j=HD)))
                        dv.inc(v.memset(vtile[:, :, HD:HD + 1], 1.0))
                        dv_rope[2][ti] = dv.n

        dump(["qf", "kf", "vt"])
        if "qkvstop" in phases:
            @block.sync
            def _(sy):
                sy.wait_ge(out.h, out.n)
            return nc
        es_a3.close()

        es_ctx = es.enter_context(contextlib.ExitStack())
        ctx_s = es_ctx.enter_context(
            nc.sbuf_tensor("ctx_s", [128, NCH * S], BF16, side="right"))
        wo_s = es_ctx.enter_context(
            nc.sbuf_tensor("wo_s", [128, NCH * D], BF16, side="right"))
        dbg_src["ctx"] = ctx_s

        @block.gpsimd
        def _(g):
            pe.wait(g, pe_qkv[2][NT - 1])
            wgo.inc(g.dma_start(
                out=wo_s[:].rearrange("p (c d) -> p c d", c=NCH),
                in_=wo_h.rearrange("(c p) d -> p c d", p=128)), 16)
        wgo_wo = wgo.n

        es_a4 = es.enter_context(contextlib.ExitStack())
        SB4 = lambda name, shape, dt: es_a4.enter_context(nc.sbuf_tensor(name, shape, dt))
        PS4 = lambda name, shape, dt: es_a4.enter_context(nc.psum_tensor(name, shape, dt))
        p_s = SB4("p_s", [128, 2048], BF16)
        ctxsb_s = SB4("ctxsb", [64, 1024], F32)
        nrm_s = SB4("nrm", [64, 1024], BF16)
        zinv_s = SB4("zinv", [128, 1024], BF16)
        psS = PS4("psS", [128, 2048], F32)
        psC = [PS4(f"psC{i}", [128, 1024], F32) for i in range(2)]

        zbs = sem("zbs")
        shs = sem("shs")

        CTX_LAG = 3
        units = [(h, qh) for h in range(H) for qh in range(2)]
        sops = []
        for ui, (h, qh) in enumerate(units):
            kts = list(range(8 * (qh + 1)))
            qbase = 1024 * qh
            last_kt_b = {b: max(kt for kt in kts
                                if max(128 * kt - qbase, 0) < 512 * (b + 1))
                         for b in range(2)}
            for kt in kts:
                rs = max(128 * kt - qbase, 0)
                for b in range(2):
                    if rs >= 512 * (b + 1):
                        continue
                    sops.append(dict(
                        ui=ui, h=h, qh=qh, c=h // 2, r0=(h % 2) * 64,
                        qbase=qbase, kt=kt, b=b, rs=rs,
                        stop=(kt == last_kt_b[b]),
                        unit_first=(kt == 0 and b == 0),
                        unit_last=(kt == kts[-1] and b == 1)))

        NS = len(sops)
        exp_hist = [0] * NS
        ctxmm_hist = [0] * NS
        pe_S = [0] * NS
        pe_ctx = {}
        ac_ctxcp = {}
        dv_nrm = {}
        dv_fin = {}
        n_odd = 0
        last_odd_unit = [None]
        pending_tail = []

        def emit_S(i):
            d = sops[i]

            @block.tensor
            def _(tt):
                if i == 0:
                    ac.wait(tt, ac_rf[1][NT - 1])
                    dv.wait(tt, dv_rope[2][NT - 1])
                if i >= 4:
                    ac.wait(tt, exp_hist[i - 4])
                pso = 512 * (i % 4)
                lhsT = kf_s[d["r0"]:d["r0"] + 64,
                            2048 * d["c"] + 128 * d["kt"]:2048 * d["c"] + 128 * (d["kt"] + 1)]
                lo, hi = max(d["rs"], 512 * d["b"]), 512 * (d["b"] + 1)
                rhs = qf_s[d["r0"]:d["r0"] + 64,
                           2048 * d["c"] + d["qbase"] + lo:2048 * d["c"] + d["qbase"] + hi]
                diag = (128 * d["kt"] >= d["qbase"]) and (512 * d["b"] <= d["rs"])
                pad = lo - 512 * d["b"]
                last = tt.matmul(psS[:, pso + pad:pso + 512], lhsT, rhs, start=True,
                                 stop=not diag, tile_position=(d["r0"], 0))
                if diag:
                    last = tt.matmul(psS[:, pso + pad:pso + pad + 128], ident_b[:],
                                     trimask_b[:], start=False, stop=True)
                if pad:
                    last = tt.matmul(psS[:, pso:pso + pad], ident_b[:],
                                     negf_b[:, 0:pad], start=False, stop=True)
                pe.inc(last)
                pe_S[i] = pe.n

        def emit_exp_pair(i, single=False):
            lo = i if single else i - 1
            off = 512 * (lo % 4)
            width = 512 if single else 1024

            @block.scalar
            def _(s):
                pe.wait(s, pe_S[i])
                if lo >= 4:
                    pe.wait(s, ctxmm_hist[lo - 4])
                ac.inc(s.activation(out=p_s[:, off:off + width],
                                    in_=psS[:, off:off + width],
                                    func=AF.Exp, scale=f32(0.125)))
                for j in range(lo, i + 1):
                    exp_hist[j] = ac.n

        def emit_ctx(i):
            d = sops[i]

            @block.tensor
            def _(tt):
                ac.wait(tt, exp_hist[i])
                if d["unit_first"] and d["ui"] >= 1:
                    u0 = units[d["ui"] - 1]
                    dv.wait(tt, dv_fin[u0])
                vtile = vt_s[:, (D + H) * d["kt"] + 65 * d["h"]:
                             (D + H) * d["kt"] + 65 * (d["h"] + 1)]
                last = tt.matmul(psC[d["ui"] % 2][0:65, 512 * d["b"]:512 * (d["b"] + 1)], vtile,
                                 p_s[:, 512 * (i % 4):512 * (i % 4) + 512],
                                 start=(d["kt"] == 0), stop=d["stop"])
                pe.inc(last)
                ctxmm_hist[i] = pe.n
                if d["unit_last"]:
                    pe_ctx[(d["h"], d["qh"])] = pe.n
            if d["unit_last"]:
                emit_unit_tail(d["ui"], d["h"], d["qh"])

        def emit_unit_tail(ui, h, qh):
            nonlocal n_odd
            c, r0 = h // 2, (h % 2) * 64
            qbase = 1024 * qh
            pc = psC[ui % 2]

            @block.vector
            def _(v):
                pe.wait(v, pe_ctx[(h, qh)])
                if ui >= 1:
                    v.wait_ge(zbs.h, ui)
                with nc.allow_low_precision(reason="1/Z in bf16"):
                    dv.inc(v.reciprocal(zinv_s[64:65, :], pc[64:65, :]))
                dv.inc(v.tensor_copy(ctxsb_s[:], pc[0:64, :]))
                dv.wait(v)
            dv_recip = dv.n

            def tail_zb(dv_recip=dv_recip, pc=pc):
                @block.tensor
                def _(tt):
                    dv.wait(tt, dv_recip)
                    last = None
                    for b in range(2):
                        sl = slice(512 * b, 512 * (b + 1))
                        last = tt.matmul(pc[0:64, sl], ones_b[64:65, 0:64],
                                         zinv_s[64:65, sl],
                                         start=True, stop=True, tile_position=(64, 0))
                    zbs.inc(last)
            pending_tail.append(tail_zb)

            if r0 == 64:
                n_odd += 1
            my_odd = n_odd

            @block.vector
            def _(v):
                v.wait_ge(zbs.h, ui + 1)
                if r0 == 64 and my_odd >= 2:
                    v.wait_ge(shs.h, my_odd - 1)
                dst = (ctx_s[0:64, 2048 * c + qbase:2048 * c + qbase + 1024]
                       if r0 == 0 else nrm_s[:])
                dv.inc(v.tensor_tensor(dst, ctxsb_s[:], pc[0:64, :], ALU.mult))
                dv_nrm[(h, qh)] = dv.n
                dv_fin[(h, qh)] = dv.n

            if r0 == 64:
                def tail_shift(pc=pc):
                    @block.tensor
                    def _(tt):
                        dv.wait(tt, dv_nrm[(h, qh)])
                        last = None
                        for b in range(2):
                            sl = slice(512 * b, 512 * (b + 1))
                            last = tt.matmul(pc[64:128, sl], ident_b[0:64, 0:64],
                                             nrm_s[:, sl], start=True, stop=True,
                                             tile_position=(0, 64))
                        shs.inc(last)
                pending_tail.append(tail_shift)

                @block.vector
                def _(v, my_odd=my_odd):
                    v.wait_ge(shs.h, my_odd)
                    dv.inc(v.tensor_copy(
                        ctx_s[64:128, 2048 * c + qbase:2048 * c + qbase + 1024],
                        pc[64:128, :]))
                    dv_fin[(h, qh)] = dv.n

        for i in range(NS):
            emit_S(i)
            if pending_tail:
                for fn in pending_tail:
                    fn()
                pending_tail = []
            if i % 2 == 1:
                emit_exp_pair(i)
            elif i == NS - 1:
                emit_exp_pair(i, single=True)
            if i - CTX_LAG >= 0:
                emit_ctx(i - CTX_LAG)
        for j in range(NS - CTX_LAG, NS):
            emit_ctx(j)
        for fn in pending_tail:
            fn()
        pending_tail = []
        es_a4.close()


        dump(["ctx"])

        es_a2.close()
        esf1 = es.enter_context(contextlib.ExitStack())
        x2_s = esf1.enter_context(nc.sbuf_tensor("x2_s", [128, NT * D], F32))
        dbg_src["x2t"] = x2_s
        es_a5 = es.enter_context(contextlib.ExitStack())
        SB5 = lambda name, shape, dt: es_a5.enter_context(nc.sbuf_tensor(name, shape, dt))
        PS5 = lambda name, shape, dt: es_a5.enter_context(nc.psum_tensor(name, shape, dt))
        xin2_s = [SB5(f"xin2{i}", [128, D], F32) for i in range(2)]
        psO = [PS5(f"psO{i}", [128, D], F32) for i in range(2)]
        dv_o = [0] * NT
        ac_sq2 = [0] * NT
        for ti in range(NT):
            @block.sync
            def _(sy, ti=ti):
                if ti >= 2:
                    dv.wait(sy, dv_o[ti - 2])
                    ac.wait(sy, ac_sq2[ti - 2])
                else:
                    dv.wait(sy, dv_fin[units[-1]])
                wx[ti % 2].inc(sy.dma_start(out=xin2_s[ti % 2][:],
                                            in_=x_h[128 * ti:128 * (ti + 1), :]), 16)
            wx_x2 = wx[ti % 2].n

            @block.tensor
            def _(tt, ti=ti):
                if ti == 0:
                    wgo.wait(tt, wgo_wo)
                    for u in units:
                        dv.wait(tt, dv_fin[u])
                if ti >= 2:
                    dv.wait(tt, dv_o[ti - 2])
                ps = psO[ti % 2]
                ctxv = ctx_s[:].rearrange("p (c n) -> p c n", c=NCH)
                for c in range(NCH):
                    lhsT = ctxv[:, c, 128 * ti:128 * (ti + 1)]
                    tt.matmul(ps[:, 0:512], lhsT, wo_s[:, 1024 * c:1024 * c + 512],
                              start=(c == 0), stop=False)
                    tt.matmul(ps[:, 512:1024], lhsT,
                              wo_s[:, 1024 * c + 512:1024 * (c + 1)],
                              start=(c == 0), stop=False)
                tt.matmul(ps[:, 0:512], ones_b[0:1, :], bo_b[:, 0:512],
                          start=False, stop=True)
                pe.inc(tt.matmul(ps[:, 512:1024], ones_b[0:1, :], bo_b[:, 512:1024],
                                 start=False, stop=True))
            pe_o = pe.n

            @block.vector
            def _(v, ti=ti, wx_x2=wx_x2, pe_o=pe_o):
                pe.wait(v, pe_o)
                v.wait_ge(wx[ti % 2].h, wx_x2)
                xv = x2_s[:, D * ti:D * (ti + 1)]
                dv.inc(v.tensor_tensor(xv, psO[ti % 2][:], xin2_s[ti % 2][:], ALU.add))
                dv_o[ti] = dv.n
                dv.inc(v.reduce_sum(stat_s[:, ti:ti + 1], xv, axis=AX.X))

            @block.scalar
            def _(s, ti=ti):
                dv.wait(s, dv_o[ti])
                ac.inc(s.activation(out=xin2_s[ti % 2][:],
                                    in_=x2_s[:, D * ti:D * (ti + 1)],
                                    func=AF.Square,
                                    accum_out=stat_s[:, 16 + ti:17 + ti]))
                ac_sq2[ti] = ac.n
        pe_attn_end = pe.n
        dv_sumA2 = dv.n
        ac_sqA2 = ac.n
        dump(["x2t"])
        es_a5.close()
        es_ctx.close()

        if "ffn" in phases:
            esf = es.enter_context(contextlib.ExitStack())
            SBF = lambda name, shape, dt: esf.enter_context(nc.sbuf_tensor(name, shape, dt))
            h2f_s = SBF("h2f_s", [128, NCH * S], BF16)
            wfc_s = [SBF(f"wfc{i}", [128, NCH * D], BF16) for i in range(2)]
            wpj_s = [SBF(f"wpj{i}", [128, NCH * D], BF16) for i in range(2)]
            y_s = SBF("y_s", [128, NCH * D], BF16)
            h2sc_s = [SBF(f"h2sc{i}", [128, D], F32) for i in range(2)]
            lot_s = [SBF(f"lot{i}", [128, D], BF16) for i in range(2)]
            lof_s = [SBF(f"lof{i}", [128, NCH * 128], BF16) for i in range(2)]
            dbg_src["h2f"] = h2f_s

            es_ln2 = es.enter_context(contextlib.ExitStack())
            PSL = lambda name, shape, dt: es_ln2.enter_context(nc.psum_tensor(name, shape, dt))
            psT2h = [PSL(f"psT2h{i}", [128, NCH * 128], BF16) for i in range(2)]
            psT2l = [PSL(f"psT2l{i}", [128, NCH * 128], BF16) for i in range(2)]
            psR = PSL("psR", [128, NE], F32)


            @block.vector
            def _(v):
                dv.wait(v, dv_sumA2)
                ac.wait(v, ac_sqA2)
                dv.inc(v.tensor_scalar(stat_s[:, 32:48], stat_s[:, 0:16],
                                       float(1.0 / D), None, ALU.mult))
                dv.wait(v)
                dv.inc(v.tensor_tensor(csc_s[:, 0:16], stat_s[:, 32:48],
                                       stat_s[:, 32:48], ALU.mult))
                dv.wait(v)
                dv.inc(v.tensor_scalar(stat_s[:, 48:64], stat_s[:, 16:32],
                                       float(1.0 / D), None, ALU.mult))
                dv.wait(v)
                dv.inc(v.tensor_tensor(stat_s[:, 48:64], stat_s[:, 48:64],
                                       csc_s[:, 0:16], ALU.subtract))
                dv.wait(v)

            @block.scalar
            def _(s):
                dv.wait(s)
                ac.inc(s.activation(out=stat_s[:, 48:64], in_=stat_s[:, 48:64],
                                    func=AF.Sqrt, bias=f32(1e-5)))

            @block.vector
            def _(v):
                ac.wait(v)
                dv.inc(v.reciprocal(stat_s[:, 48:64], stat_s[:, 48:64]))
                dv.wait(v)
            dv_rstd2 = dv.n

            dv_h2f = [0] * NT
            ac_hi = [0] * NT
            pe_hiT = [0] * NT
            ac_h2f = [0] * NT
            dv_lo = [0] * NT
            pe_loT = [0] * NT
            ac_lof = [0] * NT
            dv_rawc = [0] * NT
            for ti in range(NT):
                @block.vector
                def _(v, ti=ti):
                    if ti == 0:
                        dv.wait(v, dv_rstd2)
                    elif ti >= 2:
                        ac.wait(v, ac_hi[ti - 2])
                    xv = x2_s[:, D * ti:D * (ti + 1)]
                    dv.inc(v.tensor_scalar(h2sc_s[ti % 2][:], xv,
                                           stat_s[:, 32 + ti:33 + ti],
                                           stat_s[:, 48 + ti:49 + ti],
                                           ALU.subtract, ALU.mult))
                    dv_h2f[ti] = dv.n
                    dv.inc(v.tensor_tensor(xv, xv, h2sc_s[ti % 2][:], ALU.add))

                @block.scalar
                def _(s, ti=ti):
                    dv.wait(s, dv_h2f[ti])
                    if ti >= 2:
                        pe.wait(s, pe_loT[ti - 2])
                    ac.inc(s.copy(lot_s[ti % 2][:], h2sc_s[ti % 2][:]))
                    ac_hi[ti] = ac.n

                @block.tensor
                def _(tt, ti=ti):
                    ac.wait(tt, ac_hi[ti])
                    if ti >= 2:
                        ac.wait(tt, ac_h2f[ti - 2])
                    last = None
                    for c in range(NCH):
                        last = tt.transpose(psT2h[ti % 2][:, 128 * c:128 * (c + 1)],
                                            lot_s[ti % 2][:, 128 * c:128 * (c + 1)],
                                            ident_b[:])
                    pe.inc(last)
                    pe_hiT[ti] = pe.n

                @block.scalar
                def _(s, ti=ti):
                    pe.wait(s, pe_hiT[ti])
                    o = h2f_s[:].rearrange("p (c n) -> p c n", c=NCH)[
                        :, :, 128 * ti:128 * (ti + 1)]
                    ac.inc(s.copy(o, psT2h[ti % 2][:].rearrange("p (c n) -> p c n", c=NCH)))
                    ac_h2f[ti] = ac.n

                @block.vector
                def _(v, ti=ti):
                    pe.wait(v, pe_hiT[ti])
                    dv.inc(v.tensor_tensor(lot_s[ti % 2][:], h2sc_s[ti % 2][:],
                                           lot_s[ti % 2][:], ALU.subtract))
                    dv_lo[ti] = dv.n

                @block.tensor
                def _(tt, ti=ti):
                    dv.wait(tt, dv_lo[ti])
                    if ti >= 2:
                        ac.wait(tt, ac_lof[ti - 2])
                    last = None
                    for c in range(NCH):
                        last = tt.transpose(psT2l[ti % 2][:, 128 * c:128 * (c + 1)],
                                            lot_s[ti % 2][:, 128 * c:128 * (c + 1)],
                                            ident_b[:])
                    pe.inc(last)
                    pe_loT[ti] = pe.n

                @block.scalar
                def _(s, ti=ti):
                    pe.wait(s, pe_loT[ti])
                    ac.inc(s.copy(lof_s[ti % 2][:].rearrange("p (c n) -> p c n", c=NCH),
                                  psT2l[ti % 2][:].rearrange("p (c n) -> p c n", c=NCH)))
                    ac_lof[ti] = ac.n

                @block.tensor
                def _(tt, ti=ti):
                    ac.wait(tt, ac_lof[ti])
                    if ti == 0:
                        dv.wait(tt, dv_csplit)
                    else:
                        dv.wait(tt, dv_rawc[ti - 1])
                    h2v = h2f_s[:].rearrange("p (c n) -> p c n", c=NCH)
                    lov = lof_s[ti % 2][:].rearrange("p (c n) -> p c n", c=NCH)
                    chiv = chi_s[:].rearrange("p (c e) -> p c e", c=NCH)
                    clov = clo_s[:].rearrange("p (c e) -> p c e", c=NCH)
                    last = None
                    for c in range(NCH):
                        hi_t = h2v[:, c, 128 * ti:128 * (ti + 1)]
                        tt.matmul(psR[:], hi_t, chiv[:, c], start=(c == 0), stop=False)
                        tt.matmul(psR[:], hi_t, clov[:, c], start=False, stop=False)
                        last = tt.matmul(psR[:], lov[:, c, :], chiv[:, c], start=False,
                                         stop=(c == NCH - 1))
                    pe.inc(last)
                pe_raw = pe.n

                @block.vector
                def _(v, ti=ti, pe_raw=pe_raw):
                    pe.wait(v, pe_raw)
                    dv.inc(v.tensor_copy(rawsb_s[:, NE * ti:NE * (ti + 1)], psR[:]))
                    dv_rawc[ti] = dv.n

            E3 = lambda t: t[:].rearrange("p (t e) -> p t e", e=NE)

            @block.vector
            def _(v):
                dv.wait(v, dv_rawc[NT - 1])
                rbb = rb_s[:].unsqueeze(1).broadcast_to([128, NT, NE])
                dv.inc(v.tensor_tensor(E3(bala_s), E3(rawsb_s), rbb, ALU.add))
                dv.wait(v)
                dv.inc(v.reduce_max(stat_s[:, 0:16], E3(bala_s), axis=AX.X))
                dv.wait(v)
                mx1 = stat_s[:, 0:16].unsqueeze(2).broadcast_to([128, NT, NE])
                dv.inc(v.tensor_tensor(E3(m1_s), E3(bala_s), mx1, ALU.is_equal))
                dv.wait(v)
                dv.inc(v.scalar_tensor_tensor(bala_s[:], m1_s[:], -1e30, bala_s[:],
                                              ALU.mult, ALU.add))
                dv.wait(v)
                dv.inc(v.reduce_max(stat_s[:, 16:32], E3(bala_s), axis=AX.X))
                dv.wait(v)
                mx2 = stat_s[:, 16:32].unsqueeze(2).broadcast_to([128, NT, NE])
                dv.inc(v.tensor_tensor(E3(m2_s), E3(bala_s), mx2, ALU.is_equal))
                dv.wait(v)
                r2 = r01_s[:].rearrange("p (t two) -> p t two", two=2)
                dv.inc(v.tensor_tensor(bala_s[:], rawsb_s[:], m1_s[:], ALU.mult))
                dv.wait(v)
                dv.inc(v.reduce_sum(r2[:, :, 0:1], E3(bala_s), axis=AX.X))
                dv.wait(v)
                dv.inc(v.tensor_tensor(bala_s[:], rawsb_s[:], m2_s[:], ALU.mult))
                dv.wait(v)
                dv.inc(v.reduce_sum(r2[:, :, 1:2], E3(bala_s), axis=AX.X))
                dv.wait(v)

            @block.scalar
            def _(s):
                dv.wait(s)
                ac.inc(s.activation(out=rs01_s[:], in_=r01_s[:], func=AF.Sigmoid))

            @block.vector
            def _(v):
                ac.wait(v)
                r3 = rs01_s[:].rearrange("p (t two) -> p t two", two=2)
                dv.inc(v.tensor_tensor(w1_s[:].unsqueeze(2), r3[:, :, 0:1],
                                       r3[:, :, 1:2], ALU.subtract))
                dv.wait(v)

            @block.scalar
            def _(s):
                dv.wait(s)
                ac.inc(s.activation(out=w0_s[:], in_=w1_s[:], func=AF.Sigmoid))

            @block.vector
            def _(v):
                ac.wait(v)
                dv.inc(v.tensor_scalar(w1_s[:], w0_s[:], -1.0, 1.0, ALU.mult, ALU.add))
                dv.wait(v)
                w0b = w0_s[:].unsqueeze(2).broadcast_to([128, NT, NE])
                w1b = w1_s[:].unsqueeze(2).broadcast_to([128, NT, NE])
                dv.inc(v.tensor_tensor(E3(comb_s), E3(m1_s), w0b, ALU.mult))
                dv.wait(v)
                dv.inc(v.tensor_tensor(E3(bala_s), E3(m2_s), w1b, ALU.mult))
                dv.wait(v)
                dv.inc(v.tensor_tensor(comb_s[:], comb_s[:], bala_s[:], ALU.add))
                dv.wait(v)
            dv_comb = dv.n
            dump(["h2f", "raw", "comb"])
            es_ln2.close()

            esq = es.enter_context(contextlib.ExitStack())
            PSQ = lambda name, shape, dt: esq.enter_context(nc.psum_tensor(name, shape, dt))
            psF = [PSQ(f"psF{i}", [128, D], F32) for i in range(2)]
            psP = [PSQ(f"psP{i}", [128, D], F32) for i in range(2)]

            wq_done = {}

            @block.gpsimd
            def _(g):
                pe.wait(g, pe_attn_end)
                wg[0].inc(g.dma_start(
                    out=wfc_s[0][:].rearrange("p (c f) -> p c f", c=NCH),
                    in_=wfct_h[0].rearrange("(c p) f -> p c f", p=128)[:, :, 0:1024]), 16)
                wg[0].inc(g.dma_start(
                    out=wpj_s[0][:].rearrange("p (j o) -> p j o", j=NCH),
                    in_=wprojt_h[0].rearrange("(j p) o -> p j o", p=128)[:, 0:NCH, :]), 16)
                wq_done[(0, 0)] = wg[0].n
                wg[1].inc(g.dma_start(
                    out=wfc_s[1][:].rearrange("p (c f) -> p c f", c=NCH),
                    in_=wfct_h[0].rearrange("(c p) f -> p c f", p=128)[:, :, 1024:2048]), 16)
                wg[1].inc(g.dma_start(
                    out=wpj_s[1][:].rearrange("p (j o) -> p j o", j=NCH),
                    in_=wprojt_h[0].rearrange("(j p) o -> p j o", p=128)[:, NCH:2 * NCH, :]), 16)
                wq_done[(0, 1)] = wg[1].n

            gelu_hist = {}
            evac_hist = {}
            for e in range(NEXP):
                for q in range(NQ):
                    qi = e * NQ + q
                    buf = qi % 2
                    for th in range(2):
                        for jp in range(0, NCH, 2):
                            @block.tensor
                            def _(tt, e=e, q=q, qi=qi, th=th, buf=buf, jp=jp):
                                if jp == 0 and th == 0:
                                    wg[buf].wait(tt, wq_done[(e, q)])
                                    if qi == 0:
                                        ac.wait(tt, ac_lof[NT - 1])
                                        dv.wait(tt, dv_rawc[NT - 1])
                                h2v = h2f_s[:].rearrange("p (c n) -> p c n", c=NCH)
                                for j in (jp, jp + 1):
                                    if j >= 2:
                                        key = (qi, th, j - 2)
                                    elif th == 1:
                                        key = (qi, 0, j + 6)
                                    elif qi > 0:
                                        key = (qi - 1, 1, j + 6)
                                    else:
                                        key = None
                                    if key is not None and key in gelu_hist:
                                        ac.wait(tt, gelu_hist[key])
                                    ps = psF[j % 2]
                                    last = None
                                    for cc in range(NCH):
                                        lhsT = wfc_s[buf][:, 1024 * cc + 128 * j:
                                                          1024 * cc + 128 * (j + 1)]
                                        for tc in range(2):
                                            rhs = h2v[:, cc, 1024 * th + 512 * tc:
                                                      1024 * th + 512 * (tc + 1)]
                                            last = tt.matmul(
                                                ps[:, 512 * tc:512 * (tc + 1)], lhsT, rhs,
                                                start=(cc == 0), stop=(cc == NCH - 1))
                                    pe.inc(last)
                            pe_fc_pair = pe.n

                            @block.scalar
                            def _(s, qi=qi, th=th, jp=jp, pe_fc_pair=pe_fc_pair):
                                for j in (jp, jp + 1):
                                    pe.wait(s, pe_fc_pair - (1 if j == jp else 0))
                                    ac.inc(s.activation(
                                        out=y_s[:, 1024 * j:1024 * (j + 1)],
                                        in_=psF[j % 2][:], func=AF.Gelu))
                                    gelu_hist[(qi, th, j)] = ac.n

                        for tp in range(0, 8, 2):
                            @block.tensor
                            def _(tt, qi=qi, th=th, buf=buf, tp=tp):
                                ac.wait(tt, gelu_hist[(qi, th, NCH - 1)])
                                for tloc in (tp, tp + 1):
                                    if tloc >= 2:
                                        key = (qi, th, tloc - 2)
                                    elif th == 1:
                                        key = (qi, 0, tloc + 6)
                                    elif qi > 0:
                                        key = (qi - 1, 1, tloc + 6)
                                    else:
                                        key = None
                                    if key is not None and key in evac_hist:
                                        dv.wait(tt, evac_hist[key])
                                    pps = psP[tloc % 2]
                                    last = None
                                    for j in range(NCH):
                                        lhsT = y_s[:, 1024 * j + 128 * tloc:
                                                   1024 * j + 128 * (tloc + 1)]
                                        for dh in range(2):
                                            last = tt.matmul(
                                                pps[:, 512 * dh:512 * (dh + 1)], lhsT,
                                                wpj_s[buf][:, 1024 * j + 512 * dh:
                                                           1024 * j + 512 * (dh + 1)],
                                                start=(j == 0), stop=(j == NCH - 1))
                                    pe.inc(last)
                            pe_pj_pair = pe.n

                            @block.vector
                            def _(v, e=e, qi=qi, th=th, tp=tp, pe_pj_pair=pe_pj_pair):
                                for tloc in (tp, tp + 1):
                                    pe.wait(v, pe_pj_pair - (1 if tloc == tp else 0))
                                    ti = 8 * th + tloc
                                    xv = x2_s[:, D * ti:D * (ti + 1)]
                                    pps = psP[tloc % 2]
                                    if e < 2:
                                        dv.inc(v.tensor_tensor(xv, xv, pps[:], ALU.add))
                                    else:
                                        cc = comb_s[:, NE * ti + (e - 2):
                                                    NE * ti + (e - 2) + 1]
                                        dv.inc(v.scalar_tensor_tensor(
                                            xv, pps[:], cc, xv, ALU.mult, ALU.add))
                                    evac_hist[(qi, th, tloc)] = dv.n

                            if e == NEXP - 1 and q == NQ - 1:
                                @block.sync
                                def _(sy, th=th, tp=tp):
                                    for tloc in (tp, tp + 1):
                                        ti = 8 * th + tloc
                                        dv.wait(sy, evac_hist[(NEXP * NQ - 1, th, tloc)])
                                        out.inc(sy.dma_start(
                                            out=y_h[128 * ti:128 * (ti + 1), :],
                                            in_=x2_s[:, D * ti:D * (ti + 1)]), 16)

                    nqi = qi + 2
                    if nqi < NEXP * NQ:
                        ne_, nq_ = divmod(nqi, NQ)

                        @block.gpsimd
                        def _(g, ne_=ne_, nq_=nq_, buf=buf, n=pe.n):
                            pe.wait(g, n)
                            wg[buf].inc(g.dma_start(
                                out=wfc_s[buf][:].rearrange("p (c f) -> p c f", c=NCH),
                                in_=wfct_h[ne_].rearrange("(c p) f -> p c f", p=128)[
                                    :, :, 1024 * nq_:1024 * (nq_ + 1)]), 16)
                            wg[buf].inc(g.dma_start(
                                out=wpj_s[buf][:].rearrange("p (j o) -> p j o", j=NCH),
                                in_=wprojt_h[ne_].rearrange("(j p) o -> p j o", p=128)[
                                    :, NCH * nq_:NCH * (nq_ + 1), :]), 16)
                            wq_done[(ne_, nq_)] = wg[buf].n

            esq.close()
            esf.close()

        @block.sync
        def _(sy):
            dv.wait(sy)
            ac.wait(sy)
            pe.wait(sy)
            sy.wait_ge(out.h, out.n)

    return nc


def make_tables():
    inv_freq = np.exp(np.arange(0, HD, 2, dtype=np.float32)
                      * f32(-np.log(10000.0) / HD)).astype(np.float32)
    pos = np.arange(S, dtype=np.float32)
    ang = (pos[:, None] * inv_freq[None, :]).astype(np.float32)
    sin = np.sin(ang).astype(np.float32)
    cos = np.cos(ang).astype(np.float32)
    cosw = np.zeros((128, NT * HD), np.float32)
    ssinw = np.zeros((128, NT * HD), np.float32)
    for ti in range(NT):
        c = cos[128 * ti:128 * (ti + 1)]
        s_ = sin[128 * ti:128 * (ti + 1)]
        cosw[:, HD * ti:HD * ti + 32] = c
        cosw[:, HD * ti + 32:HD * (ti + 1)] = c
        ssinw[:, HD * ti:HD * ti + 32] = -s_
        ssinw[:, HD * ti + 32:HD * (ti + 1)] = s_
    return cosw, ssinw


def prep_shared(inputs):
    cosw, ssinw = make_tables()
    wfct = np.concatenate([np.asarray(inputs["L0_ffn_Wfc_sh"]),
                           np.asarray(inputs["L0_ffn_Wfc_r"])], 0)
    wfct = np.ascontiguousarray(wfct.transpose(0, 2, 1)).astype(np.float32)
    wprojt = np.concatenate([np.asarray(inputs["L0_ffn_Wproj_sh"]),
                             np.asarray(inputs["L0_ffn_Wproj_r"])], 0)
    wprojt = np.ascontiguousarray(wprojt.transpose(0, 2, 1)).astype(np.float32)
    return dict(
        wq=np.ascontiguousarray(np.asarray(inputs["L0_attn_Wq"], np.float32)),
        wk=np.ascontiguousarray(np.asarray(inputs["L0_attn_Wk"], np.float32)),
        wv=np.ascontiguousarray(np.asarray(inputs["L0_attn_Wv"], np.float32)),
        wo=np.ascontiguousarray(np.asarray(inputs["L0_attn_Wo"], np.float32)),
        bq=np.asarray(inputs["L0_attn_bq"], np.float32).reshape(1, D),
        bk=np.asarray(inputs["L0_attn_bk"], np.float32).reshape(1, D),
        bv=np.asarray(inputs["L0_attn_bv"], np.float32).reshape(1, D),
        bo=np.asarray(inputs["L0_attn_bo"], np.float32).reshape(1, D),
        wfct=wfct, wprojt=wprojt,
        ct=np.ascontiguousarray(np.asarray(inputs["L0_ffn_centroids"], np.float32).T),
        rb=np.ascontiguousarray(np.broadcast_to(
            np.asarray(inputs["L0_ffn_routing_bias"], np.float32)[None, :], (128, NE))),
        cosw=cosw, ssinw=ssinw,
        ident=np.eye(128, dtype=np.float32),
        trimask=np.where(np.triu(np.ones((128, 128), bool)), 0.0, -1e9).astype(np.float32),
    )


def kernel(**inputs):
    nc = build_program()
    shared = prep_shared(inputs)
    xfull = np.asarray(inputs["L0_attn_x"], np.float32)
    in_maps = []
    for b in range(8):
        d = dict(shared)
        d["x"] = np.ascontiguousarray(xfull[b])
        in_maps.append(d)
    res = run_bass_kernel_spmd(nc, in_maps, core_ids=list(range(8)))
    return np.stack([np.asarray(r["y"]) for r in res.results], 0)
